# version 08
# baseline (speedup 1.0000x reference)
import contextlib
import numpy as np
import ml_dtypes
import concourse.bass as bass
import concourse.mybir as mybir
from concourse.bass_utils import run_bass_kernel_spmd

F32 = mybir.dt.float32
BF16 = mybir.dt.bfloat16
AF = mybir.ActivationFunctionType
ALU = mybir.AluOpType
AX = mybir.AxisListType

FB = 512
NFB = 4096 // FB
TH = 1024


class Ctx:
    ENGS = ("tensor", "vector", "scalar", "gpsimd", "sync")

    def __init__(self, nc):
        self.nc = nc
        self.streams = {e: [] for e in self.ENGS}
        self.sems = {}
        self.cnt = {}

    def sem(self, name, ctxstack):
        h = ctxstack.enter_context(self.nc.semaphore(name))
        self.sems[name] = h
        self.cnt[name] = 0
        return h

    def emit(self, eng, fn):
        self.streams[eng].append(fn)

    def wait(self, eng, sem, val):
        if val <= 0:
            return
        h = self.sems[sem]
        self.streams[eng].append(lambda E: E.wait_ge(h, val))

    def inc(self, sem, n=1):
        self.cnt[sem] += n
        return self.cnt[sem]

    def replay(self, block):
        for name in self.ENGS:
            ops = self.streams[name]
            def make(ops):
                def body(E):
                    for fn in ops:
                        fn(E)
                return body
            getattr(block, name)(make(ops))


def emit_moe_wave(ctx, B, NE, NT, gate=None):
    nc = ctx.nc
    NTILE = NT // 128
    NTH = NT // TH
    NG = 4 * NTH
    npairs = NE * NFB

    GE0 = ctx.cnt["GE"]
    lb2_e = {}
    t_s1g = {}
    t_s2t = {}
    v_drain = {}
    wld_pair = {}

    TS = ctx.sems["T"]

    def emit_s1(i):
        e, fb = divmod(i, NFB)
        par = i % 2
        w1T = B["w1T"][par]
        ctx.wait("tensor", "W1L%d" % par, wld_w1[i])
        ctx.wait("tensor", "GE", GE0 + NG * i)
        for g in range(NG):
            Fp, th = divmod(g, NTH)
            ph = B["phA"] if g % 2 == 0 else B["phB"]
            if g >= 2:
                ctx.wait("tensor", "GE", GE0 + NG * i + (g - 1))
            for dc in range(8):
                stat = w1T[:, dc * FB + Fp * 128: dc * FB + (Fp + 1) * 128]
                for tb in range(TH // 512):
                    mov = B["xT"][:, dc * NT + th * TH + tb * 512:
                                  dc * NT + th * TH + (tb + 1) * 512]
                    out = ph[:, tb * 512:(tb + 1) * 512]
                    inc = (dc == 7 and tb == TH // 512 - 1)
                    def mk(out, stat, mov, start, stop, inc):
                        def f(E):
                            ins = E.matmul(out, stat, mov, start=start, stop=stop)
                            if inc:
                                ins.then_inc(TS, 1)
                        return f
                    ctx.emit("tensor", mk(out, stat, mov, dc == 0, dc == 7, inc))
            t_s1g[(i, g)] = ctx.inc("T")

    def emit_s2(j):
        e, fb = divmod(j, NFB)
        par = j % 2
        w2T = B["w2T"][par]
        hid = B["hid"][j % 2]
        ctx.wait("tensor", "W2L%d" % par, wld_w2[j])
        ctx.wait("tensor", "GE", GE0 + NG * (j + 1))
        for tt in range(NTILE):
            if tt >= 2:
                need = v_drain.get((j, tt - 2))
            else:
                need = v_drain.get((j - 1, tt + NTILE - 2))
            if need:
                ctx.wait("tensor", "V", need)
            po = B["po"][tt % 2]
            for fc in range(4):
                stat = hid[:, fc * NT + tt * 128: fc * NT + (tt + 1) * 128]
                for h in range(2):
                    mov = w2T[:, fc * 1024 + h * 512: fc * 1024 + (h + 1) * 512]
                    out = po[:, h * 512:(h + 1) * 512]
                    inc = (fc == 3 and h == 1 and fb != 0)
                    def mk(out, stat, mov, start, stop, inc):
                        def f(E):
                            ins = E.matmul(out, stat, mov, start=start, stop=stop)
                            if inc:
                                ins.then_inc(TS, 1)
                        return f
                    ctx.emit("tensor", mk(out, stat, mov, fc == 0,
                                          fc == 3 and fb != 0, inc))
            if fb == 0:
                if tt == 0:
                    ctx.wait("tensor", "LB2", lb2_e[e])
                for h in range(2):
                    stat = B["ones_bf"][0:1, 0:128]
                    mov = B["b2row"][0:1, h * 512:(h + 1) * 512]
                    out = po[:, h * 512:(h + 1) * 512]
                    def mk(out, stat, mov, inc):
                        def f(E):
                            ins = E.matmul(out, stat, mov, start=False, stop=True)
                            if inc:
                                ins.then_inc(TS, 1)
                        return f
                    ctx.emit("tensor", mk(out, stat, mov, h == 1))
            t_s2t[(j, tt)] = ctx.inc("T")

    wld_w1 = {}
    wld_w2 = {}
    W1B = [ctx.cnt["W1L0"], ctx.cnt["W1L1"]]
    W2B = [ctx.cnt["W2L0"], ctx.cnt["W2L1"]]
    for i in range(npairs):
        wld_w1[i] = W1B[i % 2] + 16 * (i // 2 + 1)
        wld_w2[i] = W2B[i % 2] + 16 * (i // 2 + 1)
    LB0 = ctx.cnt["LB2"]
    for e in range(NE):
        lb2_e[e] = LB0 + 16 * (e + 1)
    V0 = ctx.cnt["V"]
    for j in range(npairs):
        for tt in range(NTILE):
            v_drain[(j, tt)] = V0 + j * NTILE + tt + 1

    for i in range(npairs):
        emit_s1(i)
        if i > 0:
            emit_s2(i - 1)
    emit_s2(npairs - 1)

    for i in range(npairs):
        e, fb = divmod(i, NFB)
        par = i % 2
        if i >= 2:
            ctx.wait("scalar", "T", t_s2t[(i - 2, NTILE - 1)])
        for g in range(NG):
            Fp, th = divmod(g, NTH)
            ph = B["phA"] if g % 2 == 0 else B["phB"]
            ctx.wait("scalar", "T", t_s1g[(i, g)])
            b1col = B["b1cols"][:, e * 32 + fb * 4 + Fp: e * 32 + fb * 4 + Fp + 1]
            out = B["hid"][par][:, Fp * NT + th * TH: Fp * NT + (th + 1) * TH]
            def mk(out, ph, b1col):
                def f(E):
                    E.activation(out, ph[:, 0:TH], AF.Gelu, bias=b1col, scale=1.0
                                 ).then_inc(ctx.sems["GE"], 1)
                return f
            ctx.emit("scalar", mk(out, ph, b1col))
            ctx.inc("GE")

    for j in range(npairs):
        e, fb = divmod(j, NFB)
        for tt in range(NTILE):
            ctx.wait("vector", "T", t_s2t[(j, tt)])
            po = B["po"][tt % 2]
            accs = B["acc"][:, tt * 1024:(tt + 1) * 1024]
            wcol = B["w_route"][:, tt * 16 + e: tt * 16 + e + 1]
            def mk(accs, po, wcol):
                def f(E):
                    E.scalar_tensor_tensor(accs, po[:, 0:1024], wcol, accs,
                                           op0=ALU.mult, op1=ALU.add
                                           ).then_inc(ctx.sems["V"], 1)
                return f
            ctx.emit("vector", mk(accs, po, wcol))
            assert ctx.inc("V") == v_drain[(j, tt)]

    def w1_dma(i):
        e, fb = divmod(i, NFB)
        w1T = B["w1T"][i % 2]
        w1src = B["w1r_hbm"][e, fb].rearrange("p dc f -> p (dc f)")
        sem = ctx.sems["W1L%d" % (i % 2)]
        def f1(E):
            E.dma_start(out=w1T[:], in_=w1src).then_inc(sem, 16)
        ctx.emit("gpsimd", f1)
        ctx.inc("W1L%d" % (i % 2), 16)

    def w2_dma(i):
        e, fb = divmod(i, NFB)
        w2T = B["w2T"][i % 2]
        w2src = B["w2r_hbm"][e, fb].rearrange("p fc d -> p (fc d)")
        sem = ctx.sems["W2L%d" % (i % 2)]
        def f2(E):
            E.dma_start(out=w2T[:], in_=w2src).then_inc(sem, 16)
        ctx.emit("gpsimd", f2)
        ctx.inc("W2L%d" % (i % 2), 16)

    def b2_dma(e):
        sem = ctx.sems["LB2"]
        dst = B["b2row"]
        def f(E):
            E.dma_start(out=dst[0:1, :], in_=B["b2_hbm"][0:1, e * 1024:(e + 1) * 1024]
                        ).then_inc(sem, 16)
        ctx.emit("gpsimd", f)
        ctx.inc("LB2", 16)

    if gate is not None:
        ctx.wait("gpsimd", "T", gate)
    for i in range(min(2, npairs)):
        w1_dma(i)
        w2_dma(i)
    b2_dma(0)
    for i in range(2, npairs):
        ctx.wait("gpsimd", "T", t_s1g[(i - 2, NG - 1)])
        w1_dma(i)
        ctx.wait("gpsimd", "T", t_s2t[(i - 2, NTILE - 1)])
        w2_dma(i)
        if i % NFB == 2:
            e = (i - 2) // NFB + 1
            if e < NE:
                b2_dma(e)

    return t_s1g, t_s2t, v_drain, npairs


class TT:
    ENG_SEM = {"tensor": "T", "vector": "V", "scalar": "GE", "gpsimd": "G",
               "sync": "G"}

    def __init__(self, ctx):
        self.ctx = ctx
        self.reg = {}
        self.cov = {}
        self._rec = None

    def _cover(self, eng, sem, cnt):
        c = self.cov.setdefault(eng, {})
        if c.get(sem, 0) >= cnt:
            return False
        c[sem] = cnt
        return True

    def op(self, eng, fn, reads=(), writes=(), sem=None, inc=1):
        if self._rec is not None:
            self._rec.append((eng, fn, tuple(reads), tuple(writes), sem, inc))
            return None
        return self._op_real(eng, fn, reads, writes, sem, inc)

    def _op_real(self, eng, fn, reads=(), writes=(), sem=None, inc=1):
        ctx = self.ctx
        sem = sem or self.ENG_SEM[eng]
        rd = [t for t in reads if not t.startswith("pb")]
        writes = list(writes) + [t for t in reads if t.startswith("pb")]
        reads = rd
        need = {}
        def add(h):
            s, c = h
            need[s] = max(need.get(s, 0), c)
        for t in reads:
            e = self.reg.get(t)
            if e and e["w"]:
                add(e["w"])
        for t in writes:
            e = self.reg.get(t)
            if e:
                if e["w"]:
                    add(e["w"])
                for h in e["r"]:
                    add(h)
        for s, c in sorted(need.items()):
            if self._cover(eng, s, c):
                ctx.wait(eng, s, c)
        semh = ctx.sems[sem]
        def wrapped(E, fn=fn, semh=semh, inc=inc):
            fn(E).then_inc(semh, inc)
        ctx.emit(eng, wrapped)
        cnt = ctx.inc(sem, inc)
        for t in writes:
            self.reg[t] = {"w": (sem, cnt), "r": []}
        for t in reads:
            self.reg.setdefault(t, {"w": None, "r": []})["r"].append((sem, cnt))
        return (sem, cnt)

    def note_write(self, token, sem, cnt):
        self.reg[token] = {"w": (sem, cnt), "r": []}

    def note_read(self, token, sem, cnt):
        self.reg.setdefault(token, {"w": None, "r": []})["r"].append((sem, cnt))


def emit_attn_wave(tt, B, wave, NT=2048, load_tile=None, store_prev=None):
    ctx = tt.ctx
    P_ = B["pb"]
    NTILE = NT // 128
    st = B["stats"]
    SUM = lambda ti: st[:, ti:ti + 1]
    SSQ = lambda ti: st[:, 16 + ti:16 + ti + 1]
    MC = lambda ti: st[:, 32 + ti:32 + ti + 1]
    RS = lambda ti: st[:, 48 + ti:48 + ti + 1]
    SK = lambda ti: st[:, 64 + ti:64 + ti + 1]
    BS = lambda ti: st[:, 80 + ti:80 + ti + 1]
    LA = lambda ti: B["la_all"][:, ti:ti + 1]

    A = lambda ti: "acc%d" % ti
    pb = lambda i: "pb%d" % i

    eye = B["eye_f32"]
    triu = B["triu_f32"]
    trilS = B["trilS_f32"]
    ones = B["ones_row"]

    def a_stats(ti):
        xt = B["acc"][:, ti * 1024:(ti + 1) * 1024]
        tt.op("vector", lambda E, xt=xt, ti=ti: E.tensor_reduce(
            SUM(ti), xt, axis=AX.X, op=ALU.add),
            reads=[A(ti)], writes=["sum%d" % ti])
        tt.op("scalar", lambda E, xt=xt: E.activation(
            P_(2)[:, 0:512], xt[:, 0:512], AF.Square),
            reads=[A(ti)], writes=[pb(2)])
        tt.op("vector", lambda E, ti=ti: E.tensor_reduce(
            SSQ(ti), P_(2)[:, 0:512], axis=AX.X, op=ALU.add),
            reads=[pb(2)], writes=["ssq%d" % ti])
        tt.op("scalar", lambda E, xt=xt: E.activation(
            P_(3)[:, 0:512], xt[:, 512:1024], AF.Square),
            reads=[A(ti)], writes=[pb(3)])
        tt.op("vector", lambda E: E.tensor_reduce(
            B["c_tmp"][:, 5:6], P_(3)[:, 0:512], axis=AX.X, op=ALU.add),
            reads=[pb(3)], writes=["csq"])
        tt.op("vector", lambda E, ti=ti: E.tensor_tensor(
            SSQ(ti), SSQ(ti), B["c_tmp"][:, 5:6], op=ALU.add),
            reads=["csq", "ssq%d" % ti], writes=["ssq%d" % ti])

    def a_final(ti):
        tt.op("vector", lambda E, ti=ti: E.tensor_scalar(
            MC(ti), SUM(ti), 1.0 / 1024, None, op0=ALU.mult),
            reads=["sum%d" % ti], writes=["mc%d" % ti])
        tt.op("vector", lambda E, ti=ti: E.tensor_tensor(
            RS(ti), MC(ti), MC(ti), op=ALU.mult),
            reads=["mc%d" % ti], writes=["rstd%d" % ti])
        tt.op("vector", lambda E, ti=ti: E.tensor_scalar(
            SSQ(ti), SSQ(ti), 1.0 / 1024, RS(ti), op0=ALU.mult,
            op1=ALU.subtract),
            reads=["ssq%d" % ti, "rstd%d" % ti], writes=["ssq%d" % ti])
        tt.op("scalar", lambda E, ti=ti: E.activation(
            RS(ti), SSQ(ti), AF.Ln, bias=B["c_tmp"][:, 4:5]),
            reads=["ssq%d" % ti, "c_eps"], writes=["rstd%d" % ti])
        tt.op("vector", lambda E, ti=ti: E.tensor_scalar(
            RS(ti), RS(ti), -0.5, None, op0=ALU.mult),
            reads=["rstd%d" % ti], writes=["rstd%d" % ti])
        tt.op("scalar", lambda E, ti=ti: E.activation(
            RS(ti), RS(ti), AF.Exp),
            reads=["rstd%d" % ti], writes=["rstd%d" % ti])

    onesbf = B["ones_bf"]

    def a_part1(ti):
        if store_prev is not None:
            store_prev(ti)
        load_tile(ti)
        a_stats(ti)
        a_final(ti)
        pp = ti % 2
        xt = B["acc"][:, ti * 1024:(ti + 1) * 1024]
        lnb = B["ln1bf"][pp]
        lnT = B["lnT"][pp]
        tt.op("vector", lambda E, xt=xt, lnb=lnb, ti=ti: E.tensor_scalar(
            lnb[:], xt, MC(ti), RS(ti), op0=ALU.subtract, op1=ALU.mult),
            reads=[A(ti), "mc%d" % ti, "rstd%d" % ti], writes=["ln1bf%d" % pp])
        tt.op("sync", lambda E, lnb=lnb, lnT=lnT: E.dma_start(
            out=lnT[:].rearrange("p (dc t) -> p dc t", dc=8), in_=lnb[:],
            transpose=True),
            reads=["ln1bf%d" % pp], writes=["lnT%d" % pp, "XPOSE"],
            sem="TLN%d" % pp, inc=16)

    def a_part2(ti):
        pp = ti % 2
        lnT = B["lnT"][pp]
        bq = 4 * pp
        def proj(E, lnT=lnT, bq=bq):
            for dc in range(8):
                E.matmul(P_(bq)[:, 0:512],
                         lnT[:, dc * 128:(dc + 1) * 128],
                         B["Wqkvg"][:, dc * 512:(dc + 1) * 512],
                         start=(dc == 0), stop=False)
            return E.matmul(P_(bq)[:, 0:512], onesbf[0:1, 0:128],
                            B["bqkvg_row"][0:1, 0:512], start=False, stop=True)
        tt.op("tensor", proj, reads=["lnT%d" % pp, "w_attn", "c_onesbf"],
              writes=[pb(bq)])
        def projab(E, lnT=lnT, bq=bq):
            for dc in range(8):
                E.matmul(P_(bq + 1)[:, 0:2],
                         lnT[:, dc * 128:(dc + 1) * 128],
                         B["Wab"][:, dc * 2:(dc + 1) * 2],
                         start=(dc == 0), stop=False)
            return E.matmul(P_(bq + 1)[:, 0:2], onesbf[0:1, 0:128],
                            B["bab_row"][0:1, 0:2], start=False, stop=True)
        tt.op("tensor", projab, reads=["lnT%d" % pp, "w_attn", "c_onesbf"],
              writes=[pb(bq + 1)])
        q = P_(bq)[:, 0:128]
        k = P_(bq)[:, 128:256]
        v = P_(bq)[:, 256:384]
        g = P_(bq)[:, 384:512]
        qb = B["qbf"][pp]
        tt.op("vector", lambda E, q=q, qb=qb: E.tensor_copy(qb[:], q),
              reads=[pb(bq)], writes=["qbf%d" % pp])
        tt.op("sync", lambda E, qb=qb, ti=ti: E.dma_start(
            out=B["QTw"][:, ti * 128:(ti + 1) * 128], in_=qb[:],
            transpose=True),
            reads=["qbf%d" % pp], writes=["QTw_buf", "XPOSE"],
            sem="TQT", inc=16)
        tt.op("vector", lambda E, k=k, ti=ti: E.tensor_copy(
            B["KNw"][:, ti * 128:(ti + 1) * 128], k),
            reads=[pb(bq)], writes=["KN%d" % ti])
        tt.op("vector", lambda E, v=v, ti=ti: E.tensor_copy(
            B["Vw"][:, ti * 128:(ti + 1) * 128], v),
            reads=[pb(bq)], writes=["Vw%d" % ti])
        tt.op("scalar", lambda E, g=g, ti=ti: E.copy(
            B["Gw"][:, ti * 128:(ti + 1) * 128], g),
            reads=[pb(bq)], writes=["Gw%d" % ti])
        tt.op("scalar", lambda E, k=k: E.activation(
            B["ksq_scr"], k, AF.Square),
            reads=[pb(bq)], writes=["ksq_scr"])
        tt.op("vector", lambda E, ti=ti: E.tensor_reduce(
            SK(ti), B["ksq_scr"], axis=AX.X, op=ALU.add),
            reads=["ksq_scr"], writes=["ssk%d" % ti])
        tt.op("vector", lambda E, ti=ti, bq=bq: E.tensor_copy(
            B["ab_raw"][:, ti * 2:(ti + 1) * 2], P_(bq + 1)[:, 0:2]),
            reads=[pb(bq + 1)], writes=["ab%d" % ti])

    for ti in range(NTILE):
        a_part1(ti)
        if ti >= 1:
            a_part2(ti - 1)
    a_part2(NTILE - 1)

    tt.op("vector", lambda E: E.tensor_scalar(
        st[:, 64:80], st[:, 64:80], 1e-18, None, op0=ALU.max),
        reads=["ssk%d" % t2 for t2 in range(NTILE)], writes=["rnorm"])
    tt.op("scalar", lambda E: E.activation(st[:, 64:80], st[:, 64:80], AF.Ln),
          reads=["rnorm"], writes=["rnorm"])
    tt.op("vector", lambda E: E.tensor_scalar(
        st[:, 64:80], st[:, 64:80], -0.5, None, op0=ALU.mult),
        reads=["rnorm"], writes=["rnorm"])
    tt.op("scalar", lambda E: E.activation(
        st[:, 64:80], st[:, 64:80], AF.Exp),
        reads=["rnorm"], writes=["rnorm"])
    for ti in range(NTILE):
        tt.op("vector", lambda E, ti=ti: E.tensor_scalar(
            B["KNw"][:, ti * 128:(ti + 1) * 128],
            B["KNw"][:, ti * 128:(ti + 1) * 128], SK(ti), None, op0=ALU.mult),
            reads=["KN%d" % ti, "rnorm"], writes=["KN%d" % ti])
    alin = B["ab_raw"][:, 0::2]
    blin = B["ab_raw"][:, 1::2]
    abr = ["ab%d" % t2 for t2 in range(NTILE)]
    tt.op("vector", lambda E: E.tensor_scalar(
        B["v_tmp16"], alin, -1.0, None, op0=ALU.mult),
        reads=abr, writes=["vt16"])
    tt.op("vector", lambda E: E.tensor_tensor(
        B["v_tmp16"], B["v_tmp16"], alin, op=ALU.min),
        reads=abr + ["vt16"], writes=["vt16"])
    tt.op("scalar", lambda E: E.activation(
        B["v_tmp16"], B["v_tmp16"], AF.Exp),
        reads=["vt16"], writes=["vt16"])
    tt.op("vector", lambda E: E.tensor_scalar(
        B["v_tmp16"], B["v_tmp16"], 1.0, None, op0=ALU.add),
        reads=["vt16"], writes=["vt16"])
    tt.op("scalar", lambda E: E.activation(
        B["v_tmp16"], B["v_tmp16"], AF.Ln),
        reads=["vt16"], writes=["vt16"])
    tt.op("vector", lambda E: E.tensor_scalar(
        B["la_all"], alin, 0.0, None, op0=ALU.min),
        reads=abr, writes=["la"])
    tt.op("vector", lambda E: E.tensor_tensor(
        B["la_all"], B["la_all"], B["v_tmp16"], op=ALU.subtract),
        reads=["vt16", "la"], writes=["la"])
    tt.op("scalar", lambda E: E.activation(st[:, 80:96], blin, AF.Sigmoid),
          reads=abr, writes=["bsig"])
    tt.op("scalar", lambda E: E.activation(B["Gw"][:], B["Gw"][:], AF.Sigmoid),
          reads=["Gw%d" % t2 for t2 in range(NTILE)],
          writes=["Gw%d" % t2 for t2 in range(NTILE)])

    def one_chunk(ci):
        s = ci % 2
        k = ci // 2
        ti = s * 8 + k
        pi = ci % 2
        bb = 4 * pi
        sc = B["sc"][pi]
        KnT = B["KnT"][pi]
        col = B["cols"][pi]
        Stok, STtok = "S%d" % s, "ST%d" % s
        Sbtok, STbtok = "Sb%d" % s, "STb%d" % s
        Kn = B["KNw"][:, ti * 128:(ti + 1) * 128]
        QT = B["QTw"][:, ti * 128:(ti + 1) * 128]
        Gt = B["Gw"][:, ti * 128:(ti + 1) * 128]
        Vt = B["Vw"][:, ti * 128:(ti + 1) * 128]
        bcol = BS(ti)
        LGC = "lgcol%d" % pi
        P2 = lambda off=0: P_(bb + 2)

        tt.op("tensor", lambda E, ti=ti, bb=bb: E.matmul(
            P_(bb + 2)[:, 0:1], triu, LA(ti), start=True, stop=True),
            reads=["la", "consts"], writes=[pb(bb + 2)])
        tt.op("scalar", lambda E, col=col, bb=bb: E.copy(
            col["lgcol"], P_(bb + 2)[:, 0:1]),
            reads=[pb(bb + 2)], writes=[LGC])
        tt.op("tensor", lambda E, ti=ti, bb=bb: E.matmul(
            P_(bb + 2)[0:1, 0:128], LA(ti), triu, start=True, stop=True),
            reads=["la", "consts"], writes=[pb(bb + 2)])
        tt.op("scalar", lambda E, col=col, bb=bb: E.copy(
            col["lgrow"], P_(bb + 2)[0:1, 0:128]),
            reads=[pb(bb + 2)], writes=["lgrow%d" % pi])
        tt.op("vector", lambda E, col=col: E.tensor_scalar(
            col["nlgrow"], col["lgrow"], -1.0, None, op0=ALU.mult),
            reads=["lgrow%d" % pi], writes=["nlgrow%d" % pi])
        tt.op("tensor", lambda E, col=col, bb=bb: E.matmul(
            P_(bb + 1)[:, 0:128], ones[0:1, 0:128], col["nlgrow"][0:1, 0:128],
            start=True, stop=True),
            reads=["nlgrow%d" % pi], writes=[pb(bb + 1)])
        tt.op("vector", lambda E, sc=sc, col=col, bb=bb: E.tensor_scalar(
            sc["ratio"], P_(bb + 1)[:, 0:128], col["lgcol"], 0.0,
            op0=ALU.add, op1=ALU.min),
            reads=[pb(bb + 1), LGC], writes=["ratio%d" % pi])
        tt.op("scalar", lambda E, sc=sc: E.activation(
            sc["ratio"], sc["ratio"], AF.Exp),
            reads=["ratio%d" % pi], writes=["ratio%d" % pi])
        tt.op("tensor", lambda E, col=col, bb=bb: E.matmul(
            P_(bb + 1)[:, 0:128], ones[0:1, 0:128], col["lgrow"][0:1, 0:128],
            start=True, stop=True),
            reads=["lgrow%d" % pi], writes=[pb(bb + 1)])
        tt.op("vector", lambda E, sc=sc, col=col, bb=bb: E.tensor_scalar(
            sc["ratioT"], P_(bb + 1)[:, 0:128], col["lgcol"], 0.0,
            op0=ALU.subtract, op1=ALU.min),
            reads=[pb(bb + 1), LGC], writes=["ratioT%d" % pi])
        tt.op("scalar", lambda E, sc=sc: E.activation(
            sc["ratioT"], sc["ratioT"], AF.Exp),
            reads=["ratioT%d" % pi], writes=["ratioT%d" % pi])
        tt.op("scalar", lambda E, col=col: E.activation(
            col["gamcol"], col["lgcol"], AF.Exp),
            reads=[LGC], writes=["gamcol%d" % pi])
        tt.op("vector", lambda E, col=col, bcol=bcol: E.tensor_scalar(
            col["nbgam"], col["gamcol"], bcol, -1.0,
            op0=ALU.mult, op1=ALU.mult),
            reads=["gamcol%d" % pi, "bsig"], writes=["nbgam%d" % pi])
        tt.op("tensor", lambda E, col=col, bb=bb: E.matmul(
            P_(bb + 2)[:, 0:1], ones[0:1, 0:128], col["lgrow"][0:1, 127:128],
            start=True, stop=True),
            reads=["lgrow%d" % pi], writes=[pb(bb + 2)])
        tt.op("scalar", lambda E, col=col, bb=bb: E.activation(
            col["gamL"], P_(bb + 2)[:, 0:1], AF.Exp),
            reads=[pb(bb + 2)], writes=["gamL%d" % pi])
        tt.op("vector", lambda E, col=col, bb=bb: E.tensor_scalar(
            col["glover"], P_(bb + 2)[:, 0:1], col["lgcol"], 0.0,
            op0=ALU.subtract, op1=ALU.min),
            reads=[pb(bb + 2), LGC], writes=["glover%d" % pi])
        tt.op("scalar", lambda E, col=col: E.activation(
            col["glover"], col["glover"], AF.Exp),
            reads=["glover%d" % pi], writes=["glover%d" % pi])
        tt.op("scalar", lambda E, Kn=Kn, KnT=KnT: E.dma_start(
            out=KnT[:], in_=Kn, transpose=True),
            reads=["KN%d" % ti], writes=["KnT%d" % pi, "XPOSE"],
            sem="TKN%d" % pi, inc=16)
        tt.op("tensor", lambda E, KnT=KnT, bb=bb: E.matmul(
            P_(bb)[:, 0:128], KnT[:], KnT[:], start=True, stop=True),
            reads=["KnT%d" % pi], writes=[pb(bb)])
        tt.op("vector", lambda E, sc=sc, bb=bb: E.tensor_tensor(
            sc["m1"], P_(bb)[:, 0:128], trilS, op=ALU.mult),
            reads=[pb(bb), "consts"], writes=["m1_%d" % pi])
        tt.op("vector", lambda E, sc=sc, bcol=bcol: E.scalar_tensor_tensor(
            sc["A"], sc["m1"], bcol, sc["ratio"], op0=ALU.mult, op1=ALU.mult),
            reads=["m1_%d" % pi, "bsig", "ratio%d" % pi], writes=["A%d" % pi])
        tt.op("tensor", lambda E, sc=sc, bb=bb: E.transpose(
            P_(bb + 2)[:, 0:128], sc["A"], eye),
            reads=["A%d" % pi], writes=[pb(bb + 2)])
        tt.op("scalar", lambda E, sc=sc, bb=bb: E.copy(
            sc["AT"], P_(bb + 2)[:, 0:128]),
            reads=[pb(bb + 2)], writes=["AT%d" % pi])
        tt.op("vector", lambda E, sc=sc: E.tensor_tensor(
            sc["P"], eye, sc["AT"], op=ALU.subtract),
            reads=["AT%d" % pi, "consts"], writes=["P%d" % pi])
        for j in range(6):
            lhs_tok = ("AT%d" % pi) if j == 0 else ("NT%d" % pi)
            rhs_tok = ("A%d" % pi) if j == 0 else ("N%d" % pi)
            lhs_buf = "AT" if j == 0 else "NTl"
            rhs_buf = "A" if j == 0 else "N"
            tt.op("tensor", lambda E, sc=sc, bb=bb, lb=lhs_buf, rb=rhs_buf:
                  E.matmul(P_(bb + 1)[:, 0:128], sc[lb], sc[rb],
                           start=True, stop=True),
                  reads=[lhs_tok, rhs_tok], writes=[pb(bb + 1)])
            tt.op("scalar", lambda E, sc=sc, bb=bb: E.copy(
                sc["N"], P_(bb + 1)[:, 0:128]),
                reads=[pb(bb + 1)], writes=["N%d" % pi])
            if j < 5:
                tt.op("tensor", lambda E, sc=sc, bb=bb: E.transpose(
                    P_(bb + 3)[:, 0:128], sc["N"], eye),
                    reads=["N%d" % pi], writes=[pb(bb + 3)])
                tt.op("scalar", lambda E, sc=sc, bb=bb: E.copy(
                    sc["NTl"], P_(bb + 3)[:, 0:128]),
                    reads=[pb(bb + 3)], writes=["NT%d" % pi])
            def chain(E, sc=sc, bb=bb):
                E.matmul(P_(bb + 2)[:, 0:128], sc["N"], sc["P"],
                         start=True, stop=False)
                return E.matmul(P_(bb + 2)[:, 0:128], eye, sc["P"],
                                start=False, stop=True)
            tt.op("tensor", chain, reads=["N%d" % pi, "P%d" % pi, "consts"],
                  writes=[pb(bb + 2)])
            tt.op("scalar", lambda E, sc=sc, bb=bb: E.copy(
                sc["P"], P_(bb + 2)[:, 0:128]),
                reads=[pb(bb + 2)], writes=["P%d" % pi])
        tt.op("tensor", lambda E, KnT=KnT, s=s, bb=bb: E.matmul(
            P_(bb + 3)[:, 0:128], KnT[:], B["STbf"][s][:],
            start=True, stop=True),
            reads=["KnT%d" % pi, STbtok], writes=[pb(bb + 3)])
        tt.op("vector", lambda E, sc=sc, Vt=Vt, bcol=bcol: E.tensor_scalar(
            sc["m2"], Vt, bcol, None, op0=ALU.mult),
            reads=["Vw%d" % ti, "bsig"], writes=["m2_%d" % pi])
        tt.op("vector", lambda E, sc=sc, col=col, bb=bb: E.scalar_tensor_tensor(
            sc["m3"], P_(bb + 3)[:, 0:128], col["nbgam"], sc["m2"],
            op0=ALU.mult, op1=ALU.add),
            reads=[pb(bb + 3), "nbgam%d" % pi, "m2_%d" % pi],
            writes=["m3_%d" % pi])
        tt.op("tensor", lambda E, sc=sc, bb=bb: E.matmul(
            P_(bb)[:, 128:256], sc["P"], sc["m3"], start=True, stop=True),
            reads=["P%d" % pi, "m3_%d" % pi], writes=[pb(bb)])
        tt.op("scalar", lambda E, sc=sc, bb=bb: E.copy(
            sc["U"], P_(bb)[:, 128:256]),
            reads=[pb(bb)], writes=["U%d" % pi])
        tt.op("vector", lambda E, sc=sc, col=col, pi=pi: E.tensor_scalar(
            B["Uscb"][pi][:], sc["U"], col["glover"], None, op0=ALU.mult),
            reads=["U%d" % pi, "glover%d" % pi], writes=["Usc%d" % pi])
        tt.op("tensor", lambda E, sc=sc, bb=bb: E.transpose(
            P_(bb + 2)[:, 0:128], sc["U"], eye),
            reads=["U%d" % pi], writes=[pb(bb + 2)])
        tt.op("scalar", lambda E, pi=pi, bb=bb: E.copy(
            B["UTb"][pi][:], P_(bb + 2)[:, 0:128]),
            reads=[pb(bb + 2)], writes=["UT%d" % pi])
        tt.op("tensor", lambda E, pi=pi, QT=QT, bb=bb: E.matmul(
            P_(bb + 2)[:, 128:256], B["UTb"][pi][:], QT,
            start=True, stop=True),
            reads=["UT%d" % pi, "QTw_buf"], writes=[pb(bb + 2)])
        tt.op("vector", lambda E, sc=sc, bb=bb: E.tensor_tensor(
            sc["m1"], P_(bb + 2)[:, 128:256], sc["ratioT"], op=ALU.mult),
            reads=[pb(bb + 2), "ratioT%d" % pi], writes=["m1_%d" % pi])
        tt.op("vector", lambda E, sc=sc, pi=pi: E.tensor_tensor(
            B["DmTb"][pi][:], sc["m1"], triu, op=ALU.mult),
            reads=["m1_%d" % pi, "consts"], writes=["DmT%d" % pi])
        tt.op("tensor", lambda E, QT=QT, s=s, bb=bb: E.matmul(
            P_(bb + 1)[:, 128:256], QT, B["Sbf"][s][:], start=True, stop=True),
            reads=["QTw_buf", Sbtok], writes=[pb(bb + 1)])
        tt.op("tensor", lambda E, pi=pi, Kn=Kn, bb=bb: E.matmul(
            P_(bb + 3)[:, 128:256], B["DmTb"][pi][:], Kn,
            start=True, stop=True),
            reads=["DmT%d" % pi, "KN%d" % ti], writes=[pb(bb + 3)])
        tt.op("vector", lambda E, sc=sc, col=col, bb=bb: E.tensor_scalar(
            sc["m2"], P_(bb + 1)[:, 128:256], col["gamcol"], None,
            op0=ALU.mult),
            reads=[pb(bb + 1), "gamcol%d" % pi], writes=["m2_%d" % pi])
        tt.op("vector", lambda E, sc=sc, bb=bb: E.scalar_tensor_tensor(
            sc["m3"], P_(bb + 3)[:, 128:256], 0.0, sc["m2"],
            op0=ALU.add, op1=ALU.add),
            reads=[pb(bb + 3), "m2_%d" % pi], writes=["m3_%d" % pi])
        tt.op("vector", lambda E, sc=sc, Gt=Gt, pi=pi: E.tensor_tensor(
            B["OGb"][pi][:], sc["m3"], Gt, op=ALU.mult),
            reads=["m3_%d" % pi, "Gw%d" % ti], writes=["OG%d" % pi])
        tt.op("sync", lambda E, pi=pi: E.dma_start(
            out=B["OGTb"][pi][:], in_=B["OGb"][pi][:], transpose=True),
            reads=["OG%d" % pi], writes=["OGT%d" % pi, "XPOSE"],
            sem="TOG%d" % pi, inc=16)
        for h in range(2):
            pbank = bb + 2 + h
            def ymm(E, pi=pi, h=h, pbank=pbank):
                E.matmul(P_(pbank)[:, 0:512], B["OGTb"][pi][:],
                         B["Wo"][:, h * 512:(h + 1) * 512],
                         start=True, stop=False)
                return E.matmul(P_(pbank)[:, 0:512], B["ones_bf"][0:1, 0:128],
                                B["bo_row"][0:1, h * 512:(h + 1) * 512],
                                start=False, stop=True)
            tt.op("tensor", ymm, reads=["OGT%d" % pi, "w_attn", "c_onesbf"],
                  writes=[pb(pbank)])
            tt.op("vector", lambda E, ti=ti, h=h, pbank=pbank: E.tensor_tensor(
                B["acc"][:, ti * 1024 + h * 512: ti * 1024 + (h + 1) * 512],
                B["acc"][:, ti * 1024 + h * 512: ti * 1024 + (h + 1) * 512],
                P_(pbank)[:, 0:512], op=ALU.add),
                reads=[pb(pbank), A(ti)], writes=[A(ti)])
        tt.op("tensor", lambda E, pi=pi, Kn=Kn, bb=bb: E.matmul(
            P_(bb)[:, 256:384], B["Uscb"][pi][:], Kn, start=True, stop=True),
            reads=["Usc%d" % pi, "KN%d" % ti], writes=[pb(bb)])
        tt.op("tensor", lambda E, pi=pi, Kn=Kn, bb=bb: E.matmul(
            P_(bb + 3)[:, 0:128], Kn, B["Uscb"][pi][:], start=True, stop=True),
            reads=["Usc%d" % pi, "KN%d" % ti], writes=[pb(bb + 3)])
        tt.op("vector", lambda E, s=s, col=col, bb=bb: E.scalar_tensor_tensor(
            B["S"][s][:], B["S"][s][:], col["gamL"], P_(bb)[:, 256:384],
            op0=ALU.mult, op1=ALU.add),
            reads=[pb(bb), "gamL%d" % pi, Stok], writes=[Stok])
        tt.op("vector", lambda E, s=s, col=col, bb=bb: E.scalar_tensor_tensor(
            B["ST"][s][:], B["ST"][s][:], col["gamL"], P_(bb + 3)[:, 0:128],
            op0=ALU.mult, op1=ALU.add),
            reads=[pb(bb + 3), "gamL%d" % pi, STtok], writes=[STtok])
        tt.op("scalar", lambda E, s=s: E.copy(
            B["Sbf"][s][:], B["S"][s][:]),
            reads=[Stok], writes=[Sbtok])
        tt.op("scalar", lambda E, s=s: E.copy(
            B["STbf"][s][:], B["ST"][s][:]),
            reads=[STtok], writes=[STbtok])

    def d_ln(ti):
        a_stats(ti)
        a_final(ti)
        pp = ti // 8
        xt = B["acc"][:, ti * 1024:(ti + 1) * 1024]
        lnb = B["ln1bf"][pp]
        tt.op("vector", lambda E, xt=xt, lnb=lnb, ti=ti: E.tensor_scalar(
            lnb[:], xt, MC(ti), RS(ti), op0=ALU.subtract, op1=ALU.mult),
            reads=[A(ti), "mc%d" % ti, "rstd%d" % ti], writes=["ln1bf%d" % pp])
        tt.op("sync", lambda E, lnb=lnb, ti=ti: E.dma_start(
            out=B["xT"][:].rearrange("p (dc t) -> p dc t", dc=8)
                [:, :, ti * 128:(ti + 1) * 128],
            in_=lnb[:], transpose=True),
            reads=["ln1bf%d" % pp], writes=["xT_%d" % ti, "XPOSE"],
            sem="TXT%d" % pp, inc=16)

    def d_router(ti):
        pp = ti // 8
        bq = 4 * pp
        def lmm(E, ti=ti, bq=bq):
            for dc in range(8):
                E.matmul(P_(bq)[:, 0:16],
                         B["xT"][:, dc * NT + ti * 128: dc * NT + (ti + 1) * 128],
                         B["RT_bf"][:, dc * 16:(dc + 1) * 16],
                         start=(dc == 0), stop=False)
            return E.matmul(P_(bq)[:, 0:16], ones[0:1, 0:128],
                            B["cvec_row"][0:1, 0:16], start=False, stop=True)
        tt.op("tensor", lmm, reads=["xT_%d" % ti, "router_w"], writes=[pb(bq)])
        m8 = B["m8"]
        tt.op("vector", lambda E, bq=bq, m8=m8: E.max(m8[:], P_(bq)[:, 0:16]),
              reads=[pb(bq)], writes=["m8"])
        tt.op("vector", lambda E, m8=m8: E.tensor_scalar(
            B["c_tmp"][:, 0:1], m8[:, 0:1], -1.0, None, op0=ALU.mult),
            reads=["m8"], writes=["ctmp0"])
        tt.op("vector", lambda E, bq=bq, m8=m8: E.tensor_scalar(
            B["v_tmp16"], P_(bq)[:, 0:16], m8[:, 3:4], None, op0=ALU.is_ge),
            reads=[pb(bq), "m8"], writes=["vt16"])
        tt.op("scalar", lambda E, bq=bq: E.activation(
            B["v_tmp16b"], P_(bq)[:, 0:16], AF.Exp, bias=B["c_tmp"][:, 0:1]),
            reads=[pb(bq), "ctmp0"], writes=["vt16b"])
        tt.op("vector", lambda E: E.tensor_tensor(
            B["v_tmp16b"], B["v_tmp16b"], B["v_tmp16"], op=ALU.mult),
            reads=["vt16", "vt16b"], writes=["vt16b"])
        tt.op("vector", lambda E: E.tensor_reduce(
            B["c_tmp"][:, 0:1], B["v_tmp16b"], axis=AX.X, op=ALU.add),
            reads=["vt16b"], writes=["ctmp0"])
        tt.op("vector", lambda E: E.reciprocal(
            B["c_tmp"][:, 1:2], B["c_tmp"][:, 0:1]),
            reads=["ctmp0"], writes=["ctmp1"])
        tt.op("vector", lambda E, ti=ti: E.tensor_scalar(
            B["w_route"][:, ti * 16:(ti + 1) * 16], B["v_tmp16b"],
            B["c_tmp"][:, 1:2], None, op0=ALU.mult),
            reads=["vt16b", "ctmp1"], writes=["wrt%d" % ti])

    from itertools import zip_longest
    for kpair in range(NTILE // 2):
        tt._rec = []
        one_chunk(2 * kpair)
        L0 = tt._rec
        tt._rec = []
        one_chunk(2 * kpair + 1)
        L1 = tt._rec
        tt._rec = None
        for xo, yo in zip_longest(L0, L1):
            if xo is not None:
                tt._op_real(*xo)
            if yo is not None:
                tt._op_real(*yo)
        d_ln(kpair)
        d_ln(8 + kpair)
        if kpair >= 1:
            d_router(kpair - 1)
            d_router(8 + kpair - 1)
    d_router(NTILE // 2 - 1)
    d_router(NTILE - 1)



NT = 2048
NTILE = 16


def build(NE=16):
    nc = bass.Bass("TRN2", target_bir_lowering=False)
    d_x = nc.dram_tensor("x", [2, 2048, 1024], F32, kind="ExternalInput").ap()
    d_w1r = nc.dram_tensor("w1r", [NE, 8, 128, 8, 512], F32, kind="ExternalInput").ap()
    d_w2r = nc.dram_tensor("w2r", [NE, 8, 128, 4, 1024], F32, kind="ExternalInput").ap()
    d_b1 = nc.dram_tensor("b1", [NE, 32, 128], F32, kind="ExternalInput").ap()
    d_b2 = nc.dram_tensor("b2", [1, NE * 1024], F32, kind="ExternalInput").ap()
    d_wqkvg = nc.dram_tensor("wqkvg", [128, 8, 512], F32, kind="ExternalInput").ap()
    d_wab = nc.dram_tensor("wab", [128, 8, 2], F32, kind="ExternalInput").ap()
    d_wo = nc.dram_tensor("wo", [128, 1024], F32, kind="ExternalInput").ap()
    d_bqkvg = nc.dram_tensor("bqkvg", [1, 512], F32, kind="ExternalInput").ap()
    d_bab = nc.dram_tensor("bab", [1, 2], F32, kind="ExternalInput").ap()
    d_bo = nc.dram_tensor("bo", [1, 1024], F32, kind="ExternalInput").ap()
    d_keys = nc.dram_tensor("keys", [16, 1024], F32, kind="ExternalInput").ap()
    d_rw = nc.dram_tensor("rw", [16, 1024], F32, kind="ExternalInput").ap()
    d_rb = nc.dram_tensor("rb", [1, 16], F32, kind="ExternalInput").ap()
    d_out = nc.dram_tensor("out", [2, 2048, 1024], F32, kind="ExternalOutput").ap()

    with contextlib.ExitStack() as es:
        E_ = es.enter_context
        acc = E_(nc.sbuf_tensor("acc", [128, NTILE * 1024], F32))
        xT = E_(nc.sbuf_tensor("xT_sb", [128, 8 * NT], BF16))
        w1T = [E_(nc.sbuf_tensor(f"w1T{p}", [128, 8 * FB], BF16)) for p in range(2)]
        w2T = [E_(nc.sbuf_tensor(f"w2T{p}", [128, 4 * 1024], BF16)) for p in range(2)]
        arena = E_(nc.sbuf_tensor("arena", [128, 16384], BF16))
        hid = [arena[:][:, 0:8192], arena[:][:, 8192:16384]]
        KNw = arena[:][:, 0:2048]
        QTw = arena[:][:, 2048:4096]
        Gw = arena[:][:, 4096:6144]
        Vw = arena[:][:, 6144:8192]
        arenaF = arena[:].bitcast(F32)
        sc = []
        scr_names = ("ratio", "ratioT", "A", "AT", "P", "N", "NTl", "U",
                     "m1", "m2", "m3")
        for p in range(2):
            d = {}
            for i, nmm in enumerate(scr_names):
                base = 4096 + (p * len(scr_names) + i) * 128
                d[nmm] = arenaF[:, base:base + 128]
            sc.append(d)
        bfn = ("KnT", "UTb", "Uscb", "DmTb", "OGb", "OGTb")
        bft = {}
        for p in range(2):
            for i, nmm in enumerate(bfn):
                base = 13824 + (p * len(bfn) + i) * 128
                bft[(nmm, p)] = arena[:][:, base:base + 128]
        KnT = [bft[("KnT", p)] for p in range(2)]
        UTb = [bft[("UTb", p)] for p in range(2)]
        Uscb = [bft[("Uscb", p)] for p in range(2)]
        DmTb = [bft[("DmTb", p)] for p in range(2)]
        OGb = [bft[("OGb", p)] for p in range(2)]
        OGTb = [bft[("OGTb", p)] for p in range(2)]
        keysv = arenaF[0:16, 0:1024]
        rwv = arenaF[0:16, 1024:2048]
        kkv = arenaF[0:16, 2048:3072]
        b1all = arenaF[0:32, 3072:3072 + NE * 128]

        ab_raw = E_(nc.sbuf_tensor("ab_raw", [128, 32], F32))
        stats = E_(nc.sbuf_tensor("stats", [128, 96], F32))
        la_all = E_(nc.sbuf_tensor("la_all", [128, 16], F32))
        ln1bf = [E_(nc.sbuf_tensor(f"ln1bf{p}", [128, 1024], BF16)) for p in range(2)]
        lnT = [E_(nc.sbuf_tensor(f"lnT{p}", [128, 1024], BF16)) for p in range(2)]
        qbf = [E_(nc.sbuf_tensor(f"qbf{p}", [128, 128], BF16)) for p in range(2)]
        ksq_scr = E_(nc.sbuf_tensor("ksq_scr", [128, 128], F32))
        cols = []
        for p in range(2):
            d = {}
            for nmm in ("lgcol", "gamcol", "nbgam", "gamL", "glover"):
                d[nmm] = E_(nc.sbuf_tensor(f"c{p}_{nmm}", [128, 1], F32))[:]
            d["lgrow"] = E_(nc.sbuf_tensor(f"c{p}_lgrow", [1, 128], F32))[:]
            d["nlgrow"] = E_(nc.sbuf_tensor(f"c{p}_nlgrow", [1, 128], F32))[:]
            cols.append(d)
        S = [E_(nc.sbuf_tensor(f"S{s}", [128, 128], F32)) for s in range(2)]
        STt = [E_(nc.sbuf_tensor(f"STt{s}", [128, 128], F32)) for s in range(2)]
        Sbf = [E_(nc.sbuf_tensor(f"Sbf{s}", [128, 128], BF16)) for s in range(2)]
        STbf = [E_(nc.sbuf_tensor(f"STbf{s}", [128, 128], BF16)) for s in range(2)]
        eye_f32 = E_(nc.sbuf_tensor("eye_f32", [128, 128], F32))
        triu_f32 = E_(nc.sbuf_tensor("triu_f32", [128, 128], F32))
        trilS_f32 = E_(nc.sbuf_tensor("trilS_f32", [128, 128], F32))
        ones_row = E_(nc.sbuf_tensor("ones_row", [1, 128], F32))
        Wqkvg_sb = E_(nc.sbuf_tensor("Wqkvg", [128, 4096], BF16))
        Wab_sb = E_(nc.sbuf_tensor("Wab", [128, 16], BF16))
        Wo_sb = E_(nc.sbuf_tensor("Wo", [128, 1024], BF16))
        bqkvg_row = E_(nc.sbuf_tensor("bqkvg_row", [1, 512], BF16))
        bab_row = E_(nc.sbuf_tensor("bab_row", [1, 2], BF16))
        bo_row = E_(nc.sbuf_tensor("bo_row", [1, 1024], BF16))
        ones_bf = E_(nc.sbuf_tensor("ones_bf", [1, 128], BF16))
        c_tmp = E_(nc.sbuf_tensor("c_tmp", [128, 8], F32))
        v_tmp16 = E_(nc.sbuf_tensor("v_tmp16", [128, 16], F32))
        v_tmp16b = E_(nc.sbuf_tensor("v_tmp16b", [128, 16], F32))
        m8 = E_(nc.sbuf_tensor("m8", [128, 8], F32))
        RT_bf = E_(nc.sbuf_tensor("RT_bf", [128, 128], BF16))
        cvec_row = E_(nc.sbuf_tensor("cvec_row", [1, 16], F32))
        rb_row = E_(nc.sbuf_tensor("rb_row", [1, 16], F32))
        b1cols = E_(nc.sbuf_tensor("b1cols", [128, NE * 32], F32))
        b2row = E_(nc.sbuf_tensor("b2row", [1, 1024], BF16))
        wroute = E_(nc.sbuf_tensor("wroute_sb", [128, NTILE * 16], F32))
        phA = E_(nc.psum_tensor("phA", [128, 1024], F32))
        phB = E_(nc.psum_tensor("phB", [128, 1024], F32))
        po0 = E_(nc.psum_tensor("po0", [128, 1024], F32))
        po1 = E_(nc.psum_tensor("po1", [128, 1024], F32))
        pts = [phA, phB, po0, po1]

        def P_(i):
            return pts[i // 2][:, (i % 2) * 512:(i % 2 + 1) * 512]

        ctx = Ctx(nc)
        sems = ["T", "V", "GE", "G", "WA", "LX0", "LX1", "LX2", "LX3",
                "ST0", "ST1", "ST2", "ST3", "TQT",
                "TLN0", "TLN1", "TKN0", "TKN1", "TOG0", "TOG1", "TXT0", "TXT1",
                "W1L0", "W1L1", "W2L0", "W2L1", "LB1", "LB2", "KRW"]
        for s in sems:
            ctx.sem(s, es)
        block = E_(nc.Block())
        tt = TT(ctx)

        B = dict(acc=acc[:], KNw=KNw, QTw=QTw, Gw=Gw, Vw=Vw,
                 ab_raw=ab_raw[:], stats=stats[:], la_all=la_all[:],
                 ln1bf=ln1bf, lnT=lnT, qbf=qbf, sc=sc, KnT=KnT, UTb=UTb,
                 Uscb=Uscb, DmTb=DmTb, OGb=OGb, OGTb=OGTb, cols=cols,
                 S=S, ST=STt, Sbf=Sbf, STbf=STbf,
                 eye_f32=eye_f32[:], triu_f32=triu_f32[:],
                 trilS_f32=trilS_f32[:], ones_row=ones_row[:],
                 Wqkvg=Wqkvg_sb[:], Wab=Wab_sb[:], Wo=Wo_sb[:],
                 bqkvg_row=bqkvg_row[:], bab_row=bab_row[:], bo_row=bo_row[:],
                 c_tmp=c_tmp[:], v_tmp16=v_tmp16[:], v_tmp16b=v_tmp16b[:],
                 m8=m8[:], ksq_scr=ksq_scr[:], pb=P_,
                 RT_bf=RT_bf[:], cvec_row=cvec_row[:],
                 xT=xT[:], w_route=wroute[:],
                 w1T=w1T, w2T=w2T, hid=hid,
                 b1cols=b1cols[:], b2row=b2row[:], ones_bf=ones_bf[:],
                 phA=phA, phB=phB, po=[po0, po1],
                 w1r_hbm=d_w1r, w2r_hbm=d_w2r, b2_hbm=d_b2)

        tt.op("gpsimd", lambda E: E.memset(acc[:, 0:128], 1.0),
              writes=["ones_scr"])
        tt.op("gpsimd", lambda E: E.affine_select(
            eye_f32[:], acc[:, 0:128], [[-1, 128]], ALU.is_equal, 0.0,
            base=0, channel_multiplier=1),
            reads=["ones_scr"], writes=["c_eye"])
        tt.op("gpsimd", lambda E: E.affine_select(
            trilS_f32[:], acc[:, 0:128], [[-1, 128]], ALU.is_gt, 0.0,
            base=0, channel_multiplier=1),
            reads=["ones_scr"], writes=["c_tril"])
        tt.op("gpsimd", lambda E: E.affine_select(
            triu_f32[:], acc[:, 0:128], [[1, 128]], ALU.is_ge, 0.0,
            base=0, channel_multiplier=-1),
            reads=["ones_scr"], writes=["c_triu"])
        tt.op("gpsimd", lambda E: E.memset(ones_row[:], 1.0),
              writes=["c_ones"])
        tt.op("gpsimd", lambda E: E.memset(c_tmp[:, 4:5], 1e-5),
              writes=["c_eps"])
        tt.op("gpsimd", lambda E: E.memset(ones_bf[:], 1.0),
              writes=["c_onesbf"])
        tt.op("gpsimd", lambda E: E.memset(c_tmp[:, 3:4], 0.0),
              reads=["c_eye", "c_tril", "c_triu", "c_ones", "c_eps"],
              writes=["consts"])
        for s in range(2):
            tt.op("gpsimd", lambda E, s=s: E.memset(S[s][:], 0.0), writes=["S%d" % s])
            tt.op("gpsimd", lambda E, s=s: E.memset(STt[s][:], 0.0), writes=["ST%d" % s])
            tt.op("gpsimd", lambda E, s=s: E.memset(Sbf[s][:], 0.0), writes=["Sb%d" % s])
            tt.op("gpsimd", lambda E, s=s: E.memset(STbf[s][:], 0.0), writes=["STb%d" % s])
        nW = [0]
        def wdma(dst, src):
            def f(E):
                return E.dma_start(out=dst, in_=src)
            tt.op("gpsimd", f, sem="WA", inc=16)
            nW[0] += 16
        wdma(Wqkvg_sb[:].rearrange("p (dc j) -> p dc j", dc=8), d_wqkvg[:, :, :])
        wdma(Wab_sb[:].rearrange("p (dc j) -> p dc j", dc=8), d_wab[:, :, :])
        wdma(Wo_sb[:], d_wo[:, :])
        wdma(bqkvg_row[:], d_bqkvg[:, :])
        wdma(bab_row[:], d_bab[:, :])
        wdma(bo_row[:], d_bo[:, :])
        tt.note_write("w_attn", "WA", nW[0])

        def kdma(dst, src, tok):
            def f(E):
                return E.dma_start(out=dst, in_=src)
            tt.op("sync", f, writes=[tok], sem="KRW", inc=16)
        kdma(keysv, d_keys[:, :], "keysv")
        kdma(rwv, d_rw[:, :], "rwv")
        kdma(rb_row[:], d_rb[:, :], "rbv")
        tt.op("gpsimd", lambda E: E.memset(c_tmp[:, 2:3], 0.0),
              reads=["keysv", "rwv", "rbv"], writes=["krw_done"])
        for _tok in ("keysv", "rwv", "rbv"):
            tt.note_write(_tok, "G", ctx.cnt["G"])
        tt.op("vector", lambda E: E.tensor_tensor(kkv, keysv, keysv, op=ALU.mult),
              reads=["krw_done"], writes=["kkv"])
        tt.op("vector", lambda E: E.scalar_tensor_tensor(
            keysv, keysv, 2.0, rwv, op0=ALU.mult, op1=ALU.add),
            reads=["krw_done", "kkv"], writes=["keysv"])
        tt.op("vector", lambda E: E.tensor_reduce(
            rwv[:, 0:1], kkv[:, 0:1024], axis=AX.X, op=ALU.add),
            reads=["kkv", "keysv"], writes=["kredv"])
        for dc in range(8):
            tt.op("tensor", lambda E, dc=dc: E.transpose(
                P_(0)[:, 0:16], keysv[:, dc * 128:(dc + 1) * 128],
                eye_f32[0:16, 0:16]),
                reads=["keysv", "consts"], writes=["pb0"])
            tt.op("vector", lambda E, dc=dc: E.tensor_copy(
                RT_bf[:, dc * 16:(dc + 1) * 16], P_(0)[:, 0:16]),
                reads=["pb0"], writes=["router_w"])
        tt.op("tensor", lambda E: E.transpose(
            P_(0)[0:1, 0:16], rwv[0:16, 0:1], eye_f32[0:16, 0:16]),
            reads=["kredv", "consts"], writes=["pb0"])
        tt.op("vector", lambda E: E.tensor_tensor(
            cvec_row[:], rb_row[:], P_(0)[0:1, 0:16], op=ALU.subtract),
            reads=["pb0", "rbv"], writes=["router_w"])
        def fb1(E):
            return E.dma_start(
                out=b1all.rearrange("r (e c) -> r e c", e=NE),
                in_=d_b1.rearrange("e r c -> r e c"))
        tt.op("sync", fb1, writes=["b1nat"], sem="LB1", inc=16)
        for e in range(NE):
            tt.op("tensor", lambda E, e=e: E.transpose(
                P_(0)[:, 0:32], b1all[:, e * 128:(e + 1) * 128],
                eye_f32[0:32, 0:32]),
                reads=["b1nat", "consts"], writes=["pb0"])
            tt.op("vector", lambda E, e=e: E.tensor_copy(
                b1cols[:, e * 32:(e + 1) * 32], P_(0)[:, 0:32]),
                reads=["pb0"], writes=["b1cols_tok"])

        ARENA_TOKS = (["KN%d" % i for i in range(16)]
                      + ["Vw%d" % i for i in range(16)]
                      + ["Gw%d" % i for i in range(16)]
                      + ["QTw_buf", "keysv", "rwv", "kkv", "kredv", "b1nat"]
                      + ["%s%d" % (n, p) for p in range(2) for n in
                         ("ratio", "ratioT", "A", "AT", "P", "N", "NT", "U",
                          "m1_", "m2_", "m3_", "KnT", "UT", "Usc", "DmT",
                          "OG", "OGT")])

        def arena_fence(pairs):
            for tok in ARENA_TOKS:
                for sm, cn in pairs:
                    if cn > 0:
                        tt.note_read(tok, sm, cn)

        arena_fence([("V", ctx.cnt["V"]), ("T", ctx.cnt["T"]),
                     ("G", ctx.cnt["G"])])

        def load_tile(w, ti):
            s, k = divmod(ti, 8)
            g = ti % 4
            src = d_x[s, w * 1024 + k * 128: w * 1024 + (k + 1) * 128, :]
            dst = acc[:, ti * 1024:(ti + 1) * 1024]
            extra = ["consts"] if (w == 0 and ti == 0) else []
            tt.op("gpsimd", lambda E, src=src, dst=dst: E.dma_start(
                out=dst, in_=src),
                reads=extra, writes=["acc%d" % ti, "lxq%d" % g],
                sem="LX%d" % g, inc=16)

        def store_tile(w, ti):
            s, k = divmod(ti, 8)
            g = ti % 4
            dst = d_out[s, w * 1024 + k * 128: w * 1024 + (k + 1) * 128, :]
            src = acc[:, ti * 1024:(ti + 1) * 1024]
            tt.op("gpsimd", lambda E, dst=dst, src=src: E.dma_start(
                out=dst, in_=src),
                reads=["acc%d" % ti], writes=["stq%d" % g],
                sem="ST%d" % g, inc=16)

        prev_gate = None
        prev_store = None
        for w in range(2):
            emit_attn_wave(tt, B, w, NT=NT,
                           load_tile=lambda ti, w=w: load_tile(w, ti),
                           store_prev=prev_store)
            ctx.wait("tensor", "TXT0", ctx.cnt["TXT0"])
            ctx.wait("tensor", "TXT1", ctx.cnt["TXT1"])
            ctx.wait("tensor", "V", ctx.cnt["V"])
            ctx.wait("tensor", "GE", ctx.cnt["GE"])
            ctx.wait("scalar", "V", ctx.cnt["V"])
            ctx.wait("vector", "V", ctx.cnt["V"])
            ctx.wait("vector", "GE", ctx.cnt["GE"])
            t_s1g, t_s2t, v_drain, npairs = emit_moe_wave(ctx, B, NE, NT,
                                                          gate=prev_gate)
            t_last = t_s2t[(npairs - 1, NTILE - 1)]
            t_free_ph = t_s1g[(npairs - 1, 4 * (NT // TH) - 1)]
            v_last = v_drain[(npairs - 1, NTILE - 1)]
            ge_last = ctx.cnt["GE"]
            prev_gate = t_last
            for ti in range(NTILE):
                tt.note_write("acc%d" % ti, "V", v_drain[(npairs - 1, ti)])
            for ti in range(NTILE):
                tt.note_read("xT_%d" % ti, "T", t_free_ph)
            for ti in range(NTILE):
                tt.note_read("wrt%d" % ti, "V", v_last)
            tt.note_read("b1cols_tok", "GE", ge_last)
            tt.note_read("b2row_tok", "T", t_last)
            arena_fence([("T", t_last), ("V", v_last), ("GE", ge_last)])
            for i in range(4):
                tt.note_write("pb%d" % i, "T", t_free_ph)
                tt.note_read("pb%d" % i, "GE", ge_last)
            for i in range(4, 8):
                tt.note_write("pb%d" % i, "T", t_last)
                tt.note_read("pb%d" % i, "V", v_last)
                tt.note_read("pb%d" % i, "GE", ge_last)
            prev_store = lambda ti, w=w: store_tile(w, ti)
        for ti in range(NTILE):
            prev_store(ti)
        for g in range(4):
            ctx.wait("gpsimd", "ST%d" % g, ctx.cnt["ST%d" % g])
        ctx.replay(block)
    return nc


_NC_CACHE = {}


def kernel(**inputs):
    NE = 16
    x = np.ascontiguousarray(np.asarray(inputs["L0_attn_x"], np.float32))
    w1 = np.asarray(inputs["L0_ffn_w1"], np.float32)
    w2 = np.asarray(inputs["L0_ffn_w2"], np.float32)
    w1r = np.ascontiguousarray(
        w1.reshape(NE, 8, 512, 8, 128).transpose(0, 1, 4, 3, 2))
    w2r = np.ascontiguousarray(
        w2.reshape(NE, 1024, 8, 4, 128).transpose(0, 2, 4, 3, 1))
    b1 = np.ascontiguousarray(np.asarray(inputs["L0_ffn_b1"], np.float32)
                              .reshape(NE, 32, 128))
    b2 = np.ascontiguousarray(np.asarray(inputs["L0_ffn_b2"], np.float32)
                              .reshape(1, NE * 1024))
    wqkvg = np.concatenate([
        np.asarray(inputs["L0_attn_Wq"], np.float32),
        np.asarray(inputs["L0_attn_Wk"], np.float32),
        np.asarray(inputs["L0_attn_Wv"], np.float32),
        np.asarray(inputs["L0_attn_Wg"], np.float32)], axis=1)
    wqkvg_r = np.ascontiguousarray(wqkvg.reshape(8, 128, 512).transpose(1, 0, 2))
    wab = np.concatenate([np.asarray(inputs["L0_attn_Wa"], np.float32),
                          np.asarray(inputs["L0_attn_Wb"], np.float32)], axis=1)
    wab_r = np.ascontiguousarray(wab.reshape(8, 128, 2).transpose(1, 0, 2))
    wo = np.ascontiguousarray(np.asarray(inputs["L0_attn_Wo"], np.float32))
    bqkvg = np.ascontiguousarray(np.concatenate(
        [np.asarray(inputs["L0_attn_bq"], np.float32),
         np.asarray(inputs["L0_attn_bk"], np.float32),
         np.asarray(inputs["L0_attn_bv"], np.float32),
         np.asarray(inputs["L0_attn_bg"], np.float32)]).reshape(1, 512))
    bab = np.ascontiguousarray(np.concatenate(
        [np.asarray(inputs["L0_attn_ba"], np.float32),
         np.asarray(inputs["L0_attn_bb"], np.float32)]).reshape(1, 2))
    bo = np.ascontiguousarray(np.asarray(inputs["L0_attn_bo"], np.float32)
                              .reshape(1, 1024))
    keys = np.ascontiguousarray(np.asarray(inputs["L0_ffn_keys"], np.float32))
    rw = np.ascontiguousarray(np.asarray(inputs["L0_ffn_rw"], np.float32))
    rb = np.ascontiguousarray(np.asarray(inputs["L0_ffn_rb"], np.float32)
                              .reshape(1, 16))

    if "nc" not in _NC_CACHE:
        _NC_CACHE["nc"] = build(NE)
    nc = _NC_CACHE["nc"]

    shared = {"w1r": w1r, "w2r": w2r, "b1": b1, "b2": b2,
              "wqkvg": wqkvg_r, "wab": wab_r, "wo": wo, "bqkvg": bqkvg,
              "bab": bab, "bo": bo, "keys": keys, "rw": rw, "rb": rb}
    in_maps = []
    for c in range(8):
        m = dict(shared)
        m["x"] = np.ascontiguousarray(x[2 * c:2 * c + 2])
        in_maps.append(m)
    res = run_bass_kernel_spmd(nc, in_maps, core_ids=list(range(8)))
    out = np.concatenate([r["out"] for r in res.results], axis=0)
    return out


# version 09
# speedup vs baseline: 1.1161x; 1.1161x over previous
import contextlib
import numpy as np
import ml_dtypes
import concourse.bass as bass
import concourse.mybir as mybir
from concourse.bass_utils import run_bass_kernel_spmd

F32 = mybir.dt.float32
BF16 = mybir.dt.bfloat16
AF = mybir.ActivationFunctionType
ALU = mybir.AluOpType
AX = mybir.AxisListType

FB = 512
NFB = 4096 // FB
TH = 1024


class Ctx:
    ENGS = ("tensor", "vector", "scalar", "gpsimd", "sync")

    def __init__(self, nc):
        self.nc = nc
        self.streams = {e: [] for e in self.ENGS}
        self.sems = {}
        self.cnt = {}

    def sem(self, name, ctxstack):
        h = ctxstack.enter_context(self.nc.semaphore(name))
        self.sems[name] = h
        self.cnt[name] = 0
        return h

    def emit(self, eng, fn):
        self.streams[eng].append(fn)

    def wait(self, eng, sem, val):
        if val <= 0:
            return
        h = self.sems[sem]
        self.streams[eng].append(lambda E: E.wait_ge(h, val))

    def inc(self, sem, n=1):
        self.cnt[sem] += n
        return self.cnt[sem]

    def replay(self, block):
        for name in self.ENGS:
            ops = self.streams[name]
            def make(ops):
                def body(E):
                    for fn in ops:
                        fn(E)
                return body
            getattr(block, name)(make(ops))


def emit_moe_wave(ctx, B, NE, NT, gate=None):
    nc = ctx.nc
    NTILE = NT // 128
    NTH = NT // TH
    NG = 4 * NTH
    npairs = NE * NFB

    GE0 = ctx.cnt["GE"]
    lb2_e = {}
    t_s1g = {}
    t_s2t = {}
    v_drain = {}
    wld_pair = {}

    TS = ctx.sems["T"]

    def emit_s1(i):
        e, fb = divmod(i, NFB)
        par = i % 2
        w1T = B["w1T"][par]
        ctx.wait("tensor", "W1L%d" % par, wld_w1[i])
        ctx.wait("tensor", "GE", GE0 + NG * i)
        for g in range(NG):
            Fp, th = divmod(g, NTH)
            ph = B["phA"] if g % 2 == 0 else B["phB"]
            if g >= 2:
                ctx.wait("tensor", "GE", GE0 + NG * i + (g - 1))
            for dc in range(8):
                stat = w1T[:, dc * FB + Fp * 128: dc * FB + (Fp + 1) * 128]
                for tb in range(TH // 512):
                    mov = B["xT"][:, dc * NT + th * TH + tb * 512:
                                  dc * NT + th * TH + (tb + 1) * 512]
                    out = ph[:, tb * 512:(tb + 1) * 512]
                    inc = (dc == 7 and tb == TH // 512 - 1)
                    def mk(out, stat, mov, start, stop, inc):
                        def f(E):
                            ins = E.matmul(out, stat, mov, start=start, stop=stop)
                            if inc:
                                ins.then_inc(TS, 1)
                        return f
                    ctx.emit("tensor", mk(out, stat, mov, dc == 0, dc == 7, inc))
            t_s1g[(i, g)] = ctx.inc("T")

    def emit_s2(j):
        e, fb = divmod(j, NFB)
        par = j % 2
        w2T = B["w2T"][par]
        hid = B["hid"][j % 2]
        ctx.wait("tensor", "W2L%d" % par, wld_w2[j])
        ctx.wait("tensor", "GE", GE0 + NG * (j + 1))
        for tt in range(NTILE):
            if tt >= 2:
                need = v_drain.get((j, tt - 2))
            else:
                need = v_drain.get((j - 1, tt + NTILE - 2))
            if need:
                ctx.wait("tensor", "V", need)
            po = B["po"][tt % 2]
            for fc in range(4):
                stat = hid[:, fc * NT + tt * 128: fc * NT + (tt + 1) * 128]
                for h in range(2):
                    mov = w2T[:, fc * 1024 + h * 512: fc * 1024 + (h + 1) * 512]
                    out = po[:, h * 512:(h + 1) * 512]
                    inc = (fc == 3 and h == 1 and fb != 0)
                    def mk(out, stat, mov, start, stop, inc):
                        def f(E):
                            ins = E.matmul(out, stat, mov, start=start, stop=stop)
                            if inc:
                                ins.then_inc(TS, 1)
                        return f
                    ctx.emit("tensor", mk(out, stat, mov, fc == 0,
                                          fc == 3 and fb != 0, inc))
            if fb == 0:
                if tt == 0:
                    ctx.wait("tensor", "LB2", lb2_e[e])
                for h in range(2):
                    stat = B["ones_bf"][0:1, 0:128]
                    mov = B["b2row"][0:1, h * 512:(h + 1) * 512]
                    out = po[:, h * 512:(h + 1) * 512]
                    def mk(out, stat, mov, inc):
                        def f(E):
                            ins = E.matmul(out, stat, mov, start=False, stop=True)
                            if inc:
                                ins.then_inc(TS, 1)
                        return f
                    ctx.emit("tensor", mk(out, stat, mov, h == 1))
            t_s2t[(j, tt)] = ctx.inc("T")

    wld_w1 = {}
    wld_w2 = {}
    W1B = [ctx.cnt["W1L0"], ctx.cnt["W1L1"]]
    W2B = [ctx.cnt["W2L0"], ctx.cnt["W2L1"]]
    for i in range(npairs):
        wld_w1[i] = W1B[i % 2] + 16 * (i // 2 + 1)
        wld_w2[i] = W2B[i % 2] + 16 * (i // 2 + 1)
    LB0 = ctx.cnt["LB2"]
    for e in range(NE):
        lb2_e[e] = LB0 + 16 * (e + 1)
    V0 = ctx.cnt["V"]
    for j in range(npairs):
        for tt in range(NTILE):
            v_drain[(j, tt)] = V0 + j * NTILE + tt + 1

    for i in range(npairs):
        emit_s1(i)
        if i > 0:
            emit_s2(i - 1)
    emit_s2(npairs - 1)

    for i in range(npairs):
        e, fb = divmod(i, NFB)
        par = i % 2
        if i >= 2:
            ctx.wait("scalar", "T", t_s2t[(i - 2, NTILE - 1)])
        for g in range(NG):
            Fp, th = divmod(g, NTH)
            ph = B["phA"] if g % 2 == 0 else B["phB"]
            ctx.wait("scalar", "T", t_s1g[(i, g)])
            b1col = B["b1cols"][:, e * 32 + fb * 4 + Fp: e * 32 + fb * 4 + Fp + 1]
            out = B["hid"][par][:, Fp * NT + th * TH: Fp * NT + (th + 1) * TH]
            def mk(out, ph, b1col):
                def f(E):
                    E.activation(out, ph[:, 0:TH], AF.Gelu, bias=b1col, scale=1.0
                                 ).then_inc(ctx.sems["GE"], 1)
                return f
            ctx.emit("scalar", mk(out, ph, b1col))
            ctx.inc("GE")

    for j in range(npairs):
        e, fb = divmod(j, NFB)
        for tt in range(NTILE):
            ctx.wait("vector", "T", t_s2t[(j, tt)])
            po = B["po"][tt % 2]
            accs = B["acc"][:, tt * 1024:(tt + 1) * 1024]
            wcol = B["w_route"][:, tt * 16 + e: tt * 16 + e + 1]
            def mk(accs, po, wcol):
                def f(E):
                    E.scalar_tensor_tensor(accs, po[:, 0:1024], wcol, accs,
                                           op0=ALU.mult, op1=ALU.add
                                           ).then_inc(ctx.sems["V"], 1)
                return f
            ctx.emit("vector", mk(accs, po, wcol))
            assert ctx.inc("V") == v_drain[(j, tt)]

    def w1_dma(i):
        e, fb = divmod(i, NFB)
        w1T = B["w1T"][i % 2]
        w1src = B["w1r_hbm"][e, fb].rearrange("p dc f -> p (dc f)")
        sem = ctx.sems["W1L%d" % (i % 2)]
        def f1(E):
            E.dma_start(out=w1T[:], in_=w1src).then_inc(sem, 16)
        ctx.emit("gpsimd", f1)
        ctx.inc("W1L%d" % (i % 2), 16)

    def w2_dma(i):
        e, fb = divmod(i, NFB)
        w2T = B["w2T"][i % 2]
        w2src = B["w2r_hbm"][e, fb].rearrange("p fc d -> p (fc d)")
        sem = ctx.sems["W2L%d" % (i % 2)]
        def f2(E):
            E.dma_start(out=w2T[:], in_=w2src).then_inc(sem, 16)
        ctx.emit("gpsimd", f2)
        ctx.inc("W2L%d" % (i % 2), 16)

    def b2_dma(e):
        sem = ctx.sems["LB2"]
        dst = B["b2row"]
        def f(E):
            E.dma_start(out=dst[0:1, :], in_=B["b2_hbm"][0:1, e * 1024:(e + 1) * 1024]
                        ).then_inc(sem, 16)
        ctx.emit("gpsimd", f)
        ctx.inc("LB2", 16)

    if gate is not None:
        ctx.wait("gpsimd", "T", gate)
    for i in range(min(2, npairs)):
        w1_dma(i)
        w2_dma(i)
    b2_dma(0)
    for i in range(2, npairs):
        ctx.wait("gpsimd", "T", t_s1g[(i - 2, NG - 1)])
        w1_dma(i)
        ctx.wait("gpsimd", "T", t_s2t[(i - 2, NTILE - 1)])
        w2_dma(i)
        if i % NFB == 2:
            e = (i - 2) // NFB + 1
            if e < NE:
                b2_dma(e)

    return t_s1g, t_s2t, v_drain, npairs


class TT:
    ENG_SEM = {"tensor": "T", "vector": "V", "scalar": "GE", "gpsimd": "G",
               "sync": "G"}

    def __init__(self, ctx):
        self.ctx = ctx
        self.reg = {}
        self.cov = {}
        self._rec = None

    def _cover(self, eng, sem, cnt):
        c = self.cov.setdefault(eng, {})
        if c.get(sem, 0) >= cnt:
            return False
        c[sem] = cnt
        return True

    def op(self, eng, fn, reads=(), writes=(), sem=None, inc=1):
        if self._rec is not None:
            self._rec.append((eng, fn, tuple(reads), tuple(writes), sem, inc))
            return None
        return self._op_real(eng, fn, reads, writes, sem, inc)

    def _op_real(self, eng, fn, reads=(), writes=(), sem=None, inc=1):
        ctx = self.ctx
        sem = sem or self.ENG_SEM[eng]
        rd = [t for t in reads if not t.startswith("pb")]
        writes = list(writes) + [t for t in reads if t.startswith("pb")]
        reads = rd
        need = {}
        def add(h):
            s, c = h
            need[s] = max(need.get(s, 0), c)
        for t in reads:
            e = self.reg.get(t)
            if e and e["w"]:
                add(e["w"])
        for t in writes:
            e = self.reg.get(t)
            if e:
                if e["w"]:
                    add(e["w"])
                for h in e["r"]:
                    add(h)
        for s, c in sorted(need.items()):
            if self._cover(eng, s, c):
                ctx.wait(eng, s, c)
        semh = ctx.sems[sem]
        def wrapped(E, fn=fn, semh=semh, inc=inc):
            fn(E).then_inc(semh, inc)
        ctx.emit(eng, wrapped)
        cnt = ctx.inc(sem, inc)
        for t in writes:
            self.reg[t] = {"w": (sem, cnt), "r": []}
        for t in reads:
            self.reg.setdefault(t, {"w": None, "r": []})["r"].append((sem, cnt))
        return (sem, cnt)

    def note_write(self, token, sem, cnt):
        self.reg[token] = {"w": (sem, cnt), "r": []}

    def note_read(self, token, sem, cnt):
        self.reg.setdefault(token, {"w": None, "r": []})["r"].append((sem, cnt))


def emit_attn_wave(tt, B, wave, NT=2048, load_tile=None, store_prev=None):
    ctx = tt.ctx
    P_ = B["pb"]
    NTILE = NT // 128
    st = B["stats"]
    SUM = lambda ti: st[:, ti:ti + 1]
    SSQ = lambda ti: st[:, 16 + ti:16 + ti + 1]
    MC = lambda ti: st[:, 32 + ti:32 + ti + 1]
    RS = lambda ti: st[:, 48 + ti:48 + ti + 1]
    SK = lambda ti: st[:, 64 + ti:64 + ti + 1]
    BS = lambda ti: st[:, 80 + ti:80 + ti + 1]
    LA = lambda ti: B["la_all"][:, ti:ti + 1]

    A = lambda ti: "acc%d" % ti
    pb = lambda i: "pb%d" % i

    eye = B["eye_f32"]
    triu = B["triu_f32"]
    trilS = B["trilS_f32"]
    ones = B["ones_row"]

    def a_stats(ti):
        xt = B["acc"][:, ti * 1024:(ti + 1) * 1024]
        tt.op("vector", lambda E, xt=xt, ti=ti: E.tensor_reduce(
            SUM(ti), xt, axis=AX.X, op=ALU.add),
            reads=[A(ti)], writes=["sum%d" % ti])
        tt.op("scalar", lambda E, xt=xt, ti=ti: E.activation(
            B["phB"][:, 0:1024], xt, AF.Square, accum_out=SSQ(ti)),
            reads=[A(ti)], writes=[pb(2), pb(3), "ssq%d" % ti])

    def a_final(ti):
        tt.op("vector", lambda E, ti=ti: E.tensor_scalar(
            MC(ti), SUM(ti), 1.0 / 1024, None, op0=ALU.mult),
            reads=["sum%d" % ti], writes=["mc%d" % ti])
        tt.op("vector", lambda E, ti=ti: E.tensor_tensor(
            RS(ti), MC(ti), MC(ti), op=ALU.mult),
            reads=["mc%d" % ti], writes=["rstd%d" % ti])
        tt.op("vector", lambda E, ti=ti: E.tensor_scalar(
            SSQ(ti), SSQ(ti), 1.0 / 1024, RS(ti), op0=ALU.mult,
            op1=ALU.subtract),
            reads=["ssq%d" % ti, "rstd%d" % ti], writes=["ssq%d" % ti])
        tt.op("scalar", lambda E, ti=ti: E.activation(
            RS(ti), SSQ(ti), AF.Ln, bias=B["c_tmp"][:, 4:5]),
            reads=["ssq%d" % ti, "c_eps"], writes=["rstd%d" % ti])
        tt.op("scalar", lambda E, ti=ti: E.activation(
            RS(ti), RS(ti), AF.Exp, scale=-0.5),
            reads=["rstd%d" % ti], writes=["rstd%d" % ti])
        tt.op("vector", lambda E, ti=ti: E.tensor_scalar(
            SUM(ti), MC(ti), RS(ti), -1.0, op0=ALU.mult, op1=ALU.mult),
            reads=["mc%d" % ti, "rstd%d" % ti], writes=["sum%d" % ti])

    onesbf = B["ones_bf"]

    def a_part1(ti):
        if store_prev is not None:
            store_prev(ti)
        load_tile(ti)
        a_stats(ti)
        a_final(ti)
        pp = ti % 2
        xt = B["acc"][:, ti * 1024:(ti + 1) * 1024]
        lnb = B["ln1bf"][pp]
        lnT = B["lnT"][pp]
        tt.op("scalar", lambda E, xt=xt, lnb=lnb, ti=ti: E.activation(
            lnb[:], xt, AF.Identity, bias=SUM(ti), scale=RS(ti)),
            reads=[A(ti), "sum%d" % ti, "rstd%d" % ti], writes=["ln1bf%d" % pp])
        tt.op("sync", lambda E, lnb=lnb, lnT=lnT: E.dma_start(
            out=lnT[:].rearrange("p (dc t) -> p dc t", dc=8), in_=lnb[:],
            transpose=True),
            reads=["ln1bf%d" % pp], writes=["lnT%d" % pp, "XPOSE"],
            sem="TLN%d" % pp, inc=16)

    def a_part2(ti):
        pp = ti % 2
        lnT = B["lnT"][pp]
        bq = 4 * pp
        def proj(E, lnT=lnT, bq=bq):
            for dc in range(8):
                E.matmul(P_(bq)[:, 0:512],
                         lnT[:, dc * 128:(dc + 1) * 128],
                         B["Wqkvg"][:, dc * 512:(dc + 1) * 512],
                         start=(dc == 0), stop=False)
            return E.matmul(P_(bq)[:, 0:512], onesbf[0:1, 0:128],
                            B["bqkvg_row"][0:1, 0:512], start=False, stop=True)
        tt.op("tensor", proj, reads=["lnT%d" % pp, "w_attn", "c_onesbf"],
              writes=[pb(bq)])
        def projab(E, lnT=lnT, bq=bq):
            for dc in range(8):
                E.matmul(P_(bq + 1)[:, 0:2],
                         lnT[:, dc * 128:(dc + 1) * 128],
                         B["Wab"][:, dc * 2:(dc + 1) * 2],
                         start=(dc == 0), stop=False)
            return E.matmul(P_(bq + 1)[:, 0:2], onesbf[0:1, 0:128],
                            B["bab_row"][0:1, 0:2], start=False, stop=True)
        tt.op("tensor", projab, reads=["lnT%d" % pp, "w_attn", "c_onesbf"],
              writes=[pb(bq + 1)])
        q = P_(bq)[:, 0:128]
        k = P_(bq)[:, 128:256]
        v = P_(bq)[:, 256:384]
        g = P_(bq)[:, 384:512]
        qb = B["qbf"][pp]
        tt.op("vector", lambda E, q=q, qb=qb: E.tensor_copy(qb[:], q),
              reads=[pb(bq)], writes=["qbf%d" % pp])
        tt.op("sync", lambda E, qb=qb, ti=ti: E.dma_start(
            out=B["QTw"][:, ti * 128:(ti + 1) * 128], in_=qb[:],
            transpose=True),
            reads=["qbf%d" % pp], writes=["QTw_buf", "XPOSE"],
            sem="TQT", inc=16)
        tt.op("vector", lambda E, k=k, ti=ti: E.tensor_copy(
            B["KNw"][:, ti * 128:(ti + 1) * 128], k),
            reads=[pb(bq)], writes=["KN%d" % ti])
        tt.op("vector", lambda E, v=v, ti=ti: E.tensor_copy(
            B["Vw"][:, ti * 128:(ti + 1) * 128], v),
            reads=[pb(bq)], writes=["Vw%d" % ti])
        tt.op("scalar", lambda E, g=g, ti=ti: E.copy(
            B["Gw"][:, ti * 128:(ti + 1) * 128], g),
            reads=[pb(bq)], writes=["Gw%d" % ti])
        tt.op("scalar", lambda E, k=k, ti=ti: E.activation(
            B["ksq_scr"], k, AF.Square, accum_out=SK(ti)),
            reads=[pb(bq)], writes=["ksq_scr", "ssk%d" % ti])
        tt.op("vector", lambda E, ti=ti, bq=bq: E.tensor_copy(
            B["ab_raw"][:, ti * 2:(ti + 1) * 2], P_(bq + 1)[:, 0:2]),
            reads=[pb(bq + 1)], writes=["ab%d" % ti])

    tgate = [0]
    for ti in range(NTILE):
        a_part1(ti)
        if ti >= 1:
            a_part2(ti - 1)
        if ti == 4:
            tgate[0] = ctx.cnt["T"]
    a_part2(NTILE - 1)

    tt.op("vector", lambda E: E.tensor_scalar(
        st[:, 64:80], st[:, 64:80], 1e-18, None, op0=ALU.max),
        reads=["ssk%d" % t2 for t2 in range(NTILE)], writes=["rnorm"])
    tt.op("scalar", lambda E: E.activation(st[:, 64:80], st[:, 64:80], AF.Ln),
          reads=["rnorm"], writes=["rnorm"])
    tt.op("vector", lambda E: E.tensor_scalar(
        st[:, 64:80], st[:, 64:80], -0.5, None, op0=ALU.mult),
        reads=["rnorm"], writes=["rnorm"])
    tt.op("scalar", lambda E: E.activation(
        st[:, 64:80], st[:, 64:80], AF.Exp),
        reads=["rnorm"], writes=["rnorm"])
    for ti in range(NTILE):
        tt.op("vector", lambda E, ti=ti: E.tensor_scalar(
            B["KNw"][:, ti * 128:(ti + 1) * 128],
            B["KNw"][:, ti * 128:(ti + 1) * 128], SK(ti), None, op0=ALU.mult),
            reads=["KN%d" % ti, "rnorm"], writes=["KN%d" % ti])
    alin = B["ab_raw"][:, 0::2]
    blin = B["ab_raw"][:, 1::2]
    abr = ["ab%d" % t2 for t2 in range(NTILE)]
    tt.op("vector", lambda E: E.tensor_scalar(
        B["v_tmp16"], alin, -1.0, None, op0=ALU.mult),
        reads=abr, writes=["vt16"])
    tt.op("vector", lambda E: E.tensor_tensor(
        B["v_tmp16"], B["v_tmp16"], alin, op=ALU.min),
        reads=abr + ["vt16"], writes=["vt16"])
    tt.op("scalar", lambda E: E.activation(
        B["v_tmp16"], B["v_tmp16"], AF.Exp),
        reads=["vt16"], writes=["vt16"])
    tt.op("vector", lambda E: E.tensor_scalar(
        B["v_tmp16"], B["v_tmp16"], 1.0, None, op0=ALU.add),
        reads=["vt16"], writes=["vt16"])
    tt.op("scalar", lambda E: E.activation(
        B["v_tmp16"], B["v_tmp16"], AF.Ln),
        reads=["vt16"], writes=["vt16"])
    tt.op("vector", lambda E: E.tensor_scalar(
        B["la_all"], alin, 0.0, None, op0=ALU.min),
        reads=abr, writes=["la"])
    tt.op("vector", lambda E: E.tensor_tensor(
        B["la_all"], B["la_all"], B["v_tmp16"], op=ALU.subtract),
        reads=["vt16", "la"], writes=["la"])
    tt.op("scalar", lambda E: E.activation(st[:, 80:96], blin, AF.Sigmoid),
          reads=abr, writes=["bsig"])
    tt.op("scalar", lambda E: E.activation(B["Gw"][:], B["Gw"][:], AF.Sigmoid),
          reads=["Gw%d" % t2 for t2 in range(NTILE)],
          writes=["Gw%d" % t2 for t2 in range(NTILE)])

    def one_chunk(ci):
        s = ci % 2
        k = ci // 2
        ti = s * 8 + k
        pi = ci % 2
        bb = 4 * pi
        sc = B["sc"][pi]
        KnT = B["KnT"][pi]
        col = B["cols"][pi]
        Stok, STtok = "S%d" % s, "ST%d" % s
        Sbtok, STbtok = "Sb%d" % s, "STb%d" % s
        Kn = B["KNw"][:, ti * 128:(ti + 1) * 128]
        QT = B["QTw"][:, ti * 128:(ti + 1) * 128]
        Gt = B["Gw"][:, ti * 128:(ti + 1) * 128]
        Vt = B["Vw"][:, ti * 128:(ti + 1) * 128]
        bcol = BS(ti)
        LGC = "lgcol%d" % pi
        P2 = lambda off=0: P_(bb + 2)

        tt.op("tensor", lambda E, ti=ti, bb=bb: E.matmul(
            P_(bb + 2)[:, 0:1], triu, LA(ti), start=True, stop=True),
            reads=["la", "consts"], writes=[pb(bb + 2)])
        tt.op("scalar", lambda E, col=col, bb=bb: E.copy(
            col["lgcol"], P_(bb + 2)[:, 0:1]),
            reads=[pb(bb + 2)], writes=[LGC])
        tt.op("tensor", lambda E, ti=ti, bb=bb: E.matmul(
            P_(bb + 2)[0:1, 0:128], LA(ti), triu, start=True, stop=True),
            reads=["la", "consts"], writes=[pb(bb + 2)])
        tt.op("scalar", lambda E, col=col, bb=bb: E.copy(
            col["lgrow"], P_(bb + 2)[0:1, 0:128]),
            reads=[pb(bb + 2)], writes=["lgrow%d" % pi])
        tt.op("vector", lambda E, col=col: E.tensor_scalar(
            col["nlgrow"], col["lgrow"], -1.0, None, op0=ALU.mult),
            reads=["lgrow%d" % pi], writes=["nlgrow%d" % pi])
        tt.op("tensor", lambda E, col=col, bb=bb: E.matmul(
            P_(bb + 1)[:, 0:128], ones[0:1, 0:128], col["nlgrow"][0:1, 0:128],
            start=True, stop=True),
            reads=["nlgrow%d" % pi], writes=[pb(bb + 1)])
        tt.op("vector", lambda E, sc=sc, col=col, bb=bb: E.tensor_scalar(
            sc["ratio"], P_(bb + 1)[:, 0:128], col["lgcol"], 0.0,
            op0=ALU.add, op1=ALU.min),
            reads=[pb(bb + 1), LGC], writes=["ratio%d" % pi])
        tt.op("scalar", lambda E, sc=sc: E.activation(
            sc["ratio"], sc["ratio"], AF.Exp),
            reads=["ratio%d" % pi], writes=["ratio%d" % pi])
        tt.op("tensor", lambda E, col=col, bb=bb: E.matmul(
            P_(bb + 1)[:, 0:128], ones[0:1, 0:128], col["lgrow"][0:1, 0:128],
            start=True, stop=True),
            reads=["lgrow%d" % pi], writes=[pb(bb + 1)])
        tt.op("vector", lambda E, sc=sc, col=col, bb=bb: E.tensor_scalar(
            sc["ratioT"], P_(bb + 1)[:, 0:128], col["lgcol"], 0.0,
            op0=ALU.subtract, op1=ALU.min),
            reads=[pb(bb + 1), LGC], writes=["ratioT%d" % pi])
        tt.op("scalar", lambda E, sc=sc: E.activation(
            sc["ratioT"], sc["ratioT"], AF.Exp),
            reads=["ratioT%d" % pi], writes=["ratioT%d" % pi])
        tt.op("scalar", lambda E, col=col: E.activation(
            col["gamcol"], col["lgcol"], AF.Exp),
            reads=[LGC], writes=["gamcol%d" % pi])
        tt.op("vector", lambda E, col=col, bcol=bcol: E.tensor_scalar(
            col["nbgam"], col["gamcol"], bcol, -1.0,
            op0=ALU.mult, op1=ALU.mult),
            reads=["gamcol%d" % pi, "bsig"], writes=["nbgam%d" % pi])
        tt.op("tensor", lambda E, col=col, bb=bb: E.matmul(
            P_(bb + 2)[:, 0:1], ones[0:1, 0:128], col["lgrow"][0:1, 127:128],
            start=True, stop=True),
            reads=["lgrow%d" % pi], writes=[pb(bb + 2)])
        tt.op("scalar", lambda E, col=col, bb=bb: E.activation(
            col["gamL"], P_(bb + 2)[:, 0:1], AF.Exp),
            reads=[pb(bb + 2)], writes=["gamL%d" % pi])
        tt.op("vector", lambda E, col=col, bb=bb: E.tensor_scalar(
            col["glover"], P_(bb + 2)[:, 0:1], col["lgcol"], 0.0,
            op0=ALU.subtract, op1=ALU.min),
            reads=[pb(bb + 2), LGC], writes=["glover%d" % pi])
        tt.op("scalar", lambda E, col=col: E.activation(
            col["glover"], col["glover"], AF.Exp),
            reads=["glover%d" % pi], writes=["glover%d" % pi])
        tt.op("scalar", lambda E, Kn=Kn, KnT=KnT: E.dma_start(
            out=KnT[:], in_=Kn, transpose=True),
            reads=["KN%d" % ti], writes=["KnT%d" % pi, "XPOSE"],
            sem="TKN%d" % pi, inc=16)
        tt.op("tensor", lambda E, KnT=KnT, bb=bb: E.matmul(
            P_(bb)[:, 0:128], KnT[:], KnT[:], start=True, stop=True),
            reads=["KnT%d" % pi], writes=[pb(bb)])
        tt.op("vector", lambda E, sc=sc, bb=bb: E.tensor_tensor(
            sc["m1"], P_(bb)[:, 0:128], trilS, op=ALU.mult),
            reads=[pb(bb), "consts"], writes=["m1_%d" % pi])
        tt.op("vector", lambda E, sc=sc, bcol=bcol: E.scalar_tensor_tensor(
            sc["A"], sc["m1"], bcol, sc["ratio"], op0=ALU.mult, op1=ALU.mult),
            reads=["m1_%d" % pi, "bsig", "ratio%d" % pi], writes=["A%d" % pi])
        tt.op("tensor", lambda E, sc=sc, bb=bb: E.transpose(
            P_(bb + 2)[:, 0:128], sc["A"], eye),
            reads=["A%d" % pi], writes=[pb(bb + 2)])
        tt.op("scalar", lambda E, sc=sc, bb=bb: E.copy(
            sc["AT"], P_(bb + 2)[:, 0:128]),
            reads=[pb(bb + 2)], writes=["AT%d" % pi])
        tt.op("vector", lambda E, sc=sc: E.tensor_tensor(
            sc["P"], eye, sc["AT"], op=ALU.subtract),
            reads=["AT%d" % pi, "consts"], writes=["P%d" % pi])
        for j in range(6):
            lhs_tok = ("AT%d" % pi) if j == 0 else ("NT%d" % pi)
            rhs_tok = ("A%d" % pi) if j == 0 else ("N%d" % pi)
            lhs_buf = "AT" if j == 0 else "NTl"
            rhs_buf = "A" if j == 0 else "N"
            tt.op("tensor", lambda E, sc=sc, bb=bb, lb=lhs_buf, rb=rhs_buf:
                  E.matmul(P_(bb + 1)[:, 0:128], sc[lb], sc[rb],
                           start=True, stop=True),
                  reads=[lhs_tok, rhs_tok], writes=[pb(bb + 1)])
            tt.op("scalar", lambda E, sc=sc, bb=bb: E.copy(
                sc["N"], P_(bb + 1)[:, 0:128]),
                reads=[pb(bb + 1)], writes=["N%d" % pi])
            if j < 5:
                tt.op("tensor", lambda E, sc=sc, bb=bb: E.transpose(
                    P_(bb + 3)[:, 0:128], sc["N"], eye),
                    reads=["N%d" % pi], writes=[pb(bb + 3)])
                tt.op("scalar", lambda E, sc=sc, bb=bb: E.copy(
                    sc["NTl"], P_(bb + 3)[:, 0:128]),
                    reads=[pb(bb + 3)], writes=["NT%d" % pi])
            def chain(E, sc=sc, bb=bb):
                E.matmul(P_(bb + 2)[:, 0:128], sc["N"], sc["P"],
                         start=True, stop=False)
                return E.matmul(P_(bb + 2)[:, 0:128], eye, sc["P"],
                                start=False, stop=True)
            tt.op("tensor", chain, reads=["N%d" % pi, "P%d" % pi, "consts"],
                  writes=[pb(bb + 2)])
            tt.op("scalar", lambda E, sc=sc, bb=bb: E.copy(
                sc["P"], P_(bb + 2)[:, 0:128]),
                reads=[pb(bb + 2)], writes=["P%d" % pi])
        tt.op("tensor", lambda E, KnT=KnT, s=s, bb=bb: E.matmul(
            P_(bb + 3)[:, 0:128], KnT[:], B["STbf"][s][:],
            start=True, stop=True),
            reads=["KnT%d" % pi, STbtok], writes=[pb(bb + 3)])
        tt.op("vector", lambda E, sc=sc, Vt=Vt, bcol=bcol: E.tensor_scalar(
            sc["m2"], Vt, bcol, None, op0=ALU.mult),
            reads=["Vw%d" % ti, "bsig"], writes=["m2_%d" % pi])
        tt.op("vector", lambda E, sc=sc, col=col, bb=bb: E.scalar_tensor_tensor(
            sc["m3"], P_(bb + 3)[:, 0:128], col["nbgam"], sc["m2"],
            op0=ALU.mult, op1=ALU.add),
            reads=[pb(bb + 3), "nbgam%d" % pi, "m2_%d" % pi],
            writes=["m3_%d" % pi])
        tt.op("tensor", lambda E, sc=sc, bb=bb: E.matmul(
            P_(bb)[:, 128:256], sc["P"], sc["m3"], start=True, stop=True),
            reads=["P%d" % pi, "m3_%d" % pi], writes=[pb(bb)])
        tt.op("scalar", lambda E, sc=sc, bb=bb: E.copy(
            sc["U"], P_(bb)[:, 128:256]),
            reads=[pb(bb)], writes=["U%d" % pi])
        tt.op("vector", lambda E, sc=sc, col=col, pi=pi: E.tensor_scalar(
            B["Uscb"][pi][:], sc["U"], col["glover"], None, op0=ALU.mult),
            reads=["U%d" % pi, "glover%d" % pi], writes=["Usc%d" % pi])
        tt.op("tensor", lambda E, sc=sc, bb=bb: E.transpose(
            P_(bb + 2)[:, 0:128], sc["U"], eye),
            reads=["U%d" % pi], writes=[pb(bb + 2)])
        tt.op("scalar", lambda E, pi=pi, bb=bb: E.copy(
            B["UTb"][pi][:], P_(bb + 2)[:, 0:128]),
            reads=[pb(bb + 2)], writes=["UT%d" % pi])
        tt.op("tensor", lambda E, pi=pi, QT=QT, bb=bb: E.matmul(
            P_(bb + 2)[:, 128:256], B["UTb"][pi][:], QT,
            start=True, stop=True),
            reads=["UT%d" % pi, "QTw_buf"], writes=[pb(bb + 2)])
        tt.op("vector", lambda E, sc=sc, bb=bb: E.tensor_tensor(
            sc["m1"], P_(bb + 2)[:, 128:256], sc["ratioT"], op=ALU.mult),
            reads=[pb(bb + 2), "ratioT%d" % pi], writes=["m1_%d" % pi])
        tt.op("vector", lambda E, sc=sc, pi=pi: E.tensor_tensor(
            B["DmTb"][pi][:], sc["m1"], triu, op=ALU.mult),
            reads=["m1_%d" % pi, "consts"], writes=["DmT%d" % pi])
        tt.op("tensor", lambda E, QT=QT, s=s, bb=bb: E.matmul(
            P_(bb + 1)[:, 128:256], QT, B["Sbf"][s][:], start=True, stop=True),
            reads=["QTw_buf", Sbtok], writes=[pb(bb + 1)])
        tt.op("tensor", lambda E, pi=pi, Kn=Kn, bb=bb: E.matmul(
            P_(bb + 3)[:, 128:256], B["DmTb"][pi][:], Kn,
            start=True, stop=True),
            reads=["DmT%d" % pi, "KN%d" % ti], writes=[pb(bb + 3)])
        tt.op("vector", lambda E, sc=sc, col=col, bb=bb: E.tensor_scalar(
            sc["m2"], P_(bb + 1)[:, 128:256], col["gamcol"], None,
            op0=ALU.mult),
            reads=[pb(bb + 1), "gamcol%d" % pi], writes=["m2_%d" % pi])
        tt.op("vector", lambda E, sc=sc, bb=bb: E.scalar_tensor_tensor(
            sc["m3"], P_(bb + 3)[:, 128:256], 0.0, sc["m2"],
            op0=ALU.add, op1=ALU.add),
            reads=[pb(bb + 3), "m2_%d" % pi], writes=["m3_%d" % pi])
        tt.op("vector", lambda E, sc=sc, Gt=Gt, pi=pi: E.tensor_tensor(
            B["OGb"][pi][:], sc["m3"], Gt, op=ALU.mult),
            reads=["m3_%d" % pi, "Gw%d" % ti], writes=["OG%d" % pi])
        tt.op("sync", lambda E, pi=pi: E.dma_start(
            out=B["OGTb"][pi][:], in_=B["OGb"][pi][:], transpose=True),
            reads=["OG%d" % pi], writes=["OGT%d" % pi, "XPOSE"],
            sem="TOG%d" % pi, inc=16)
        for h in range(2):
            pbank = bb + 2 + h
            def ymm(E, pi=pi, h=h, pbank=pbank):
                E.matmul(P_(pbank)[:, 0:512], B["OGTb"][pi][:],
                         B["Wo"][:, h * 512:(h + 1) * 512],
                         start=True, stop=False)
                return E.matmul(P_(pbank)[:, 0:512], B["ones_bf"][0:1, 0:128],
                                B["bo_row"][0:1, h * 512:(h + 1) * 512],
                                start=False, stop=True)
            tt.op("tensor", ymm, reads=["OGT%d" % pi, "w_attn", "c_onesbf"],
                  writes=[pb(pbank)])
            tt.op("vector", lambda E, ti=ti, h=h, pbank=pbank: E.tensor_tensor(
                B["acc"][:, ti * 1024 + h * 512: ti * 1024 + (h + 1) * 512],
                B["acc"][:, ti * 1024 + h * 512: ti * 1024 + (h + 1) * 512],
                P_(pbank)[:, 0:512], op=ALU.add),
                reads=[pb(pbank), A(ti)], writes=[A(ti)])
        tt.op("tensor", lambda E, pi=pi, Kn=Kn, bb=bb: E.matmul(
            P_(bb)[:, 256:384], B["Uscb"][pi][:], Kn, start=True, stop=True),
            reads=["Usc%d" % pi, "KN%d" % ti], writes=[pb(bb)])
        tt.op("tensor", lambda E, pi=pi, Kn=Kn, bb=bb: E.matmul(
            P_(bb + 3)[:, 0:128], Kn, B["Uscb"][pi][:], start=True, stop=True),
            reads=["Usc%d" % pi, "KN%d" % ti], writes=[pb(bb + 3)])
        tt.op("vector", lambda E, s=s, col=col, bb=bb: E.scalar_tensor_tensor(
            B["S"][s][:], B["S"][s][:], col["gamL"], P_(bb)[:, 256:384],
            op0=ALU.mult, op1=ALU.add),
            reads=[pb(bb), "gamL%d" % pi, Stok], writes=[Stok])
        tt.op("vector", lambda E, s=s, col=col, bb=bb: E.scalar_tensor_tensor(
            B["ST"][s][:], B["ST"][s][:], col["gamL"], P_(bb + 3)[:, 0:128],
            op0=ALU.mult, op1=ALU.add),
            reads=[pb(bb + 3), "gamL%d" % pi, STtok], writes=[STtok])
        tt.op("scalar", lambda E, s=s: E.copy(
            B["Sbf"][s][:], B["S"][s][:]),
            reads=[Stok], writes=[Sbtok])
        tt.op("scalar", lambda E, s=s: E.copy(
            B["STbf"][s][:], B["ST"][s][:]),
            reads=[STtok], writes=[STbtok])

    def d_ln(ti):
        a_stats(ti)
        a_final(ti)
        pp = ti // 8
        xt = B["acc"][:, ti * 1024:(ti + 1) * 1024]
        lnb = B["ln1bf"][pp]
        tt.op("scalar", lambda E, xt=xt, lnb=lnb, ti=ti: E.activation(
            lnb[:], xt, AF.Identity, bias=SUM(ti), scale=RS(ti)),
            reads=[A(ti), "sum%d" % ti, "rstd%d" % ti], writes=["ln1bf%d" % pp])
        tt.op("sync", lambda E, lnb=lnb, ti=ti: E.dma_start(
            out=B["xT"][:].rearrange("p (dc t) -> p dc t", dc=8)
                [:, :, ti * 128:(ti + 1) * 128],
            in_=lnb[:], transpose=True),
            reads=["ln1bf%d" % pp], writes=["xT_%d" % ti, "XPOSE"],
            sem="TXT%d" % pp, inc=16)

    def d_router(ti):
        pp = ti // 8
        bq = 4 * pp
        def lmm(E, ti=ti, bq=bq):
            for dc in range(8):
                E.matmul(P_(bq)[:, 0:16],
                         B["xT"][:, dc * NT + ti * 128: dc * NT + (ti + 1) * 128],
                         B["RT_bf"][:, dc * 16:(dc + 1) * 16],
                         start=(dc == 0), stop=False)
            return E.matmul(P_(bq)[:, 0:16], ones[0:1, 0:128],
                            B["cvec_row"][0:1, 0:16], start=False, stop=True)
        tt.op("tensor", lmm, reads=["xT_%d" % ti, "router_w"], writes=[pb(bq)])
        m8 = B["m8"]
        tt.op("vector", lambda E, bq=bq, m8=m8: E.max(m8[:], P_(bq)[:, 0:16]),
              reads=[pb(bq)], writes=["m8"])
        tt.op("vector", lambda E, m8=m8: E.tensor_scalar(
            B["c_tmp"][:, 0:1], m8[:, 0:1], -1.0, None, op0=ALU.mult),
            reads=["m8"], writes=["ctmp0"])
        tt.op("vector", lambda E, bq=bq, m8=m8: E.tensor_scalar(
            B["v_tmp16"], P_(bq)[:, 0:16], m8[:, 3:4], None, op0=ALU.is_ge),
            reads=[pb(bq), "m8"], writes=["vt16"])
        tt.op("scalar", lambda E, bq=bq: E.activation(
            B["v_tmp16b"], P_(bq)[:, 0:16], AF.Exp, bias=B["c_tmp"][:, 0:1]),
            reads=[pb(bq), "ctmp0"], writes=["vt16b"])
        tt.op("vector", lambda E: E.tensor_tensor(
            B["v_tmp16b"], B["v_tmp16b"], B["v_tmp16"], op=ALU.mult),
            reads=["vt16", "vt16b"], writes=["vt16b"])
        tt.op("vector", lambda E: E.tensor_reduce(
            B["c_tmp"][:, 0:1], B["v_tmp16b"], axis=AX.X, op=ALU.add),
            reads=["vt16b"], writes=["ctmp0"])
        tt.op("vector", lambda E: E.reciprocal(
            B["c_tmp"][:, 1:2], B["c_tmp"][:, 0:1]),
            reads=["ctmp0"], writes=["ctmp1"])
        tt.op("vector", lambda E, ti=ti: E.tensor_scalar(
            B["w_route"][:, ti * 16:(ti + 1) * 16], B["v_tmp16b"],
            B["c_tmp"][:, 1:2], None, op0=ALU.mult),
            reads=["vt16b", "ctmp1"], writes=["wrt%d" % ti])

    from itertools import zip_longest
    for kpair in range(NTILE // 2):
        tt._rec = []
        one_chunk(2 * kpair)
        L0 = tt._rec
        tt._rec = []
        one_chunk(2 * kpair + 1)
        L1 = tt._rec
        tt._rec = None
        for xo, yo in zip_longest(L0, L1):
            if xo is not None:
                tt._op_real(*xo)
            if yo is not None:
                tt._op_real(*yo)
        d_ln(kpair)
        d_ln(8 + kpair)
        if kpair >= 1:
            d_router(kpair - 1)
            d_router(8 + kpair - 1)
    d_router(NTILE // 2 - 1)
    d_router(NTILE - 1)
    return tgate[0]



NT = 2048
NTILE = 16


def build(NE=16):
    nc = bass.Bass("TRN2", target_bir_lowering=False)
    d_x = nc.dram_tensor("x", [2, 2048, 1024], F32, kind="ExternalInput").ap()
    d_w1r = nc.dram_tensor("w1r", [NE, 8, 128, 8, 512], F32, kind="ExternalInput").ap()
    d_w2r = nc.dram_tensor("w2r", [NE, 8, 128, 4, 1024], F32, kind="ExternalInput").ap()
    d_b1 = nc.dram_tensor("b1", [NE, 32, 128], F32, kind="ExternalInput").ap()
    d_b2 = nc.dram_tensor("b2", [1, NE * 1024], F32, kind="ExternalInput").ap()
    d_wqkvg = nc.dram_tensor("wqkvg", [128, 8, 512], F32, kind="ExternalInput").ap()
    d_wab = nc.dram_tensor("wab", [128, 8, 2], F32, kind="ExternalInput").ap()
    d_wo = nc.dram_tensor("wo", [128, 1024], F32, kind="ExternalInput").ap()
    d_bqkvg = nc.dram_tensor("bqkvg", [1, 512], F32, kind="ExternalInput").ap()
    d_bab = nc.dram_tensor("bab", [1, 2], F32, kind="ExternalInput").ap()
    d_bo = nc.dram_tensor("bo", [1, 1024], F32, kind="ExternalInput").ap()
    d_keys = nc.dram_tensor("keys", [16, 1024], F32, kind="ExternalInput").ap()
    d_rw = nc.dram_tensor("rw", [16, 1024], F32, kind="ExternalInput").ap()
    d_rb = nc.dram_tensor("rb", [1, 16], F32, kind="ExternalInput").ap()
    d_out = nc.dram_tensor("out", [2, 2048, 1024], F32, kind="ExternalOutput").ap()

    with contextlib.ExitStack() as es:
        E_ = es.enter_context
        acc = E_(nc.sbuf_tensor("acc", [128, NTILE * 1024], F32))
        xT = E_(nc.sbuf_tensor("xT_sb", [128, 8 * NT], BF16))
        w1T = [E_(nc.sbuf_tensor(f"w1T{p}", [128, 8 * FB], BF16)) for p in range(2)]
        w2T = [E_(nc.sbuf_tensor(f"w2T{p}", [128, 4 * 1024], BF16)) for p in range(2)]
        arena = E_(nc.sbuf_tensor("arena", [128, 16384], BF16))
        hid = [arena[:][:, 0:8192], arena[:][:, 8192:16384]]
        KNw = arena[:][:, 0:2048]
        QTw = arena[:][:, 2048:4096]
        Gw = arena[:][:, 4096:6144]
        Vw = arena[:][:, 6144:8192]
        arenaF = arena[:].bitcast(F32)
        sc = []
        scr_names = ("ratio", "ratioT", "A", "AT", "P", "N", "NTl", "U",
                     "m1", "m2", "m3")
        for p in range(2):
            d = {}
            for i, nmm in enumerate(scr_names):
                base = 4096 + (p * len(scr_names) + i) * 128
                d[nmm] = arenaF[:, base:base + 128]
            sc.append(d)
        bfn = ("KnT", "UTb", "Uscb", "DmTb", "OGb", "OGTb")
        bft = {}
        for p in range(2):
            for i, nmm in enumerate(bfn):
                base = 13824 + (p * len(bfn) + i) * 128
                bft[(nmm, p)] = arena[:][:, base:base + 128]
        KnT = [bft[("KnT", p)] for p in range(2)]
        UTb = [bft[("UTb", p)] for p in range(2)]
        Uscb = [bft[("Uscb", p)] for p in range(2)]
        DmTb = [bft[("DmTb", p)] for p in range(2)]
        OGb = [bft[("OGb", p)] for p in range(2)]
        OGTb = [bft[("OGTb", p)] for p in range(2)]
        keysv = arenaF[0:16, 0:1024]
        rwv = arenaF[0:16, 1024:2048]
        kkv = arenaF[0:16, 2048:3072]
        b1all = arenaF[0:32, 3072:3072 + NE * 128]

        ab_raw = E_(nc.sbuf_tensor("ab_raw", [128, 32], F32))
        stats = E_(nc.sbuf_tensor("stats", [128, 96], F32))
        la_all = E_(nc.sbuf_tensor("la_all", [128, 16], F32))
        ln1bf = [E_(nc.sbuf_tensor(f"ln1bf{p}", [128, 1024], BF16)) for p in range(2)]
        lnT = [E_(nc.sbuf_tensor(f"lnT{p}", [128, 1024], BF16)) for p in range(2)]
        qbf = [E_(nc.sbuf_tensor(f"qbf{p}", [128, 128], BF16)) for p in range(2)]
        ksq_scr = E_(nc.sbuf_tensor("ksq_scr", [128, 128], F32))
        cols = []
        for p in range(2):
            d = {}
            for nmm in ("lgcol", "gamcol", "nbgam", "gamL", "glover"):
                d[nmm] = E_(nc.sbuf_tensor(f"c{p}_{nmm}", [128, 1], F32))[:]
            d["lgrow"] = E_(nc.sbuf_tensor(f"c{p}_lgrow", [1, 128], F32))[:]
            d["nlgrow"] = E_(nc.sbuf_tensor(f"c{p}_nlgrow", [1, 128], F32))[:]
            cols.append(d)
        S = [E_(nc.sbuf_tensor(f"S{s}", [128, 128], F32)) for s in range(2)]
        STt = [E_(nc.sbuf_tensor(f"STt{s}", [128, 128], F32)) for s in range(2)]
        Sbf = [E_(nc.sbuf_tensor(f"Sbf{s}", [128, 128], BF16)) for s in range(2)]
        STbf = [E_(nc.sbuf_tensor(f"STbf{s}", [128, 128], BF16)) for s in range(2)]
        eye_f32 = E_(nc.sbuf_tensor("eye_f32", [128, 128], F32))
        triu_f32 = E_(nc.sbuf_tensor("triu_f32", [128, 128], F32))
        trilS_f32 = E_(nc.sbuf_tensor("trilS_f32", [128, 128], F32))
        ones_row = E_(nc.sbuf_tensor("ones_row", [1, 128], F32))
        Wqkvg_sb = E_(nc.sbuf_tensor("Wqkvg", [128, 4096], BF16))
        Wab_sb = E_(nc.sbuf_tensor("Wab", [128, 16], BF16))
        Wo_sb = E_(nc.sbuf_tensor("Wo", [128, 1024], BF16))
        bqkvg_row = E_(nc.sbuf_tensor("bqkvg_row", [1, 512], BF16))
        bab_row = E_(nc.sbuf_tensor("bab_row", [1, 2], BF16))
        bo_row = E_(nc.sbuf_tensor("bo_row", [1, 1024], BF16))
        ones_bf = E_(nc.sbuf_tensor("ones_bf", [1, 128], BF16))
        c_tmp = E_(nc.sbuf_tensor("c_tmp", [128, 8], F32))
        v_tmp16 = E_(nc.sbuf_tensor("v_tmp16", [128, 16], F32))
        v_tmp16b = E_(nc.sbuf_tensor("v_tmp16b", [128, 16], F32))
        m8 = E_(nc.sbuf_tensor("m8", [128, 8], F32))
        RT_bf = E_(nc.sbuf_tensor("RT_bf", [128, 128], BF16))
        cvec_row = E_(nc.sbuf_tensor("cvec_row", [1, 16], F32))
        rb_row = E_(nc.sbuf_tensor("rb_row", [1, 16], F32))
        b1cols = E_(nc.sbuf_tensor("b1cols", [128, NE * 32], F32))
        b2row = E_(nc.sbuf_tensor("b2row", [1, 1024], BF16))
        wroute = E_(nc.sbuf_tensor("wroute_sb", [128, NTILE * 16], F32))
        phA = E_(nc.psum_tensor("phA", [128, 1024], F32))
        phB = E_(nc.psum_tensor("phB", [128, 1024], F32))
        po0 = E_(nc.psum_tensor("po0", [128, 1024], F32))
        po1 = E_(nc.psum_tensor("po1", [128, 1024], F32))
        pts = [phA, phB, po0, po1]

        def P_(i):
            return pts[i // 2][:, (i % 2) * 512:(i % 2 + 1) * 512]

        ctx = Ctx(nc)
        sems = ["T", "V", "GE", "G", "WA", "LX0", "LX1", "LX2", "LX3",
                "ST0", "ST1", "ST2", "ST3", "TQT",
                "TLN0", "TLN1", "TKN0", "TKN1", "TOG0", "TOG1", "TXT0", "TXT1",
                "W1L0", "W1L1", "W2L0", "W2L1", "LB1", "LB2", "KRW"]
        for s in sems:
            ctx.sem(s, es)
        block = E_(nc.Block())
        tt = TT(ctx)

        B = dict(acc=acc[:], KNw=KNw, QTw=QTw, Gw=Gw, Vw=Vw,
                 ab_raw=ab_raw[:], stats=stats[:], la_all=la_all[:],
                 ln1bf=ln1bf, lnT=lnT, qbf=qbf, sc=sc, KnT=KnT, UTb=UTb,
                 Uscb=Uscb, DmTb=DmTb, OGb=OGb, OGTb=OGTb, cols=cols,
                 S=S, ST=STt, Sbf=Sbf, STbf=STbf,
                 eye_f32=eye_f32[:], triu_f32=triu_f32[:],
                 trilS_f32=trilS_f32[:], ones_row=ones_row[:],
                 Wqkvg=Wqkvg_sb[:], Wab=Wab_sb[:], Wo=Wo_sb[:],
                 bqkvg_row=bqkvg_row[:], bab_row=bab_row[:], bo_row=bo_row[:],
                 c_tmp=c_tmp[:], v_tmp16=v_tmp16[:], v_tmp16b=v_tmp16b[:],
                 m8=m8[:], ksq_scr=ksq_scr[:], pb=P_,
                 RT_bf=RT_bf[:], cvec_row=cvec_row[:],
                 xT=xT[:], w_route=wroute[:],
                 w1T=w1T, w2T=w2T, hid=hid,
                 b1cols=b1cols[:], b2row=b2row[:], ones_bf=ones_bf[:],
                 phA=phA, phB=phB, po=[po0, po1],
                 w1r_hbm=d_w1r, w2r_hbm=d_w2r, b2_hbm=d_b2)

        tt.op("gpsimd", lambda E: E.memset(acc[:, 0:128], 1.0),
              writes=["ones_scr"])
        tt.op("gpsimd", lambda E: E.affine_select(
            eye_f32[:], acc[:, 0:128], [[-1, 128]], ALU.is_equal, 0.0,
            base=0, channel_multiplier=1),
            reads=["ones_scr"], writes=["c_eye"])
        tt.op("gpsimd", lambda E: E.affine_select(
            trilS_f32[:], acc[:, 0:128], [[-1, 128]], ALU.is_gt, 0.0,
            base=0, channel_multiplier=1),
            reads=["ones_scr"], writes=["c_tril"])
        tt.op("gpsimd", lambda E: E.affine_select(
            triu_f32[:], acc[:, 0:128], [[1, 128]], ALU.is_ge, 0.0,
            base=0, channel_multiplier=-1),
            reads=["ones_scr"], writes=["c_triu"])
        tt.op("gpsimd", lambda E: E.memset(ones_row[:], 1.0),
              writes=["c_ones"])
        tt.op("gpsimd", lambda E: E.memset(c_tmp[:, 4:5], 1e-5),
              writes=["c_eps"])
        tt.op("gpsimd", lambda E: E.memset(ones_bf[:], 1.0),
              writes=["c_onesbf"])
        tt.op("gpsimd", lambda E: E.memset(c_tmp[:, 3:4], 0.0),
              reads=["c_eye", "c_tril", "c_triu", "c_ones", "c_eps"],
              writes=["consts"])
        for s in range(2):
            tt.op("gpsimd", lambda E, s=s: E.memset(S[s][:], 0.0), writes=["S%d" % s])
            tt.op("gpsimd", lambda E, s=s: E.memset(STt[s][:], 0.0), writes=["ST%d" % s])
            tt.op("gpsimd", lambda E, s=s: E.memset(Sbf[s][:], 0.0), writes=["Sb%d" % s])
            tt.op("gpsimd", lambda E, s=s: E.memset(STbf[s][:], 0.0), writes=["STb%d" % s])
        nW = [0]
        def wdma(dst, src):
            def f(E):
                return E.dma_start(out=dst, in_=src)
            tt.op("gpsimd", f, sem="WA", inc=16)
            nW[0] += 16
        wdma(Wqkvg_sb[:].rearrange("p (dc j) -> p dc j", dc=8), d_wqkvg[:, :, :])
        wdma(Wab_sb[:].rearrange("p (dc j) -> p dc j", dc=8), d_wab[:, :, :])
        wdma(Wo_sb[:], d_wo[:, :])
        wdma(bqkvg_row[:], d_bqkvg[:, :])
        wdma(bab_row[:], d_bab[:, :])
        wdma(bo_row[:], d_bo[:, :])
        tt.note_write("w_attn", "WA", nW[0])

        def kdma(dst, src, tok):
            def f(E):
                return E.dma_start(out=dst, in_=src)
            tt.op("sync", f, writes=[tok], sem="KRW", inc=16)
        kdma(keysv, d_keys[:, :], "keysv")
        kdma(rwv, d_rw[:, :], "rwv")
        kdma(rb_row[:], d_rb[:, :], "rbv")
        tt.op("gpsimd", lambda E: E.memset(c_tmp[:, 2:3], 0.0),
              reads=["keysv", "rwv", "rbv"], writes=["krw_done"])
        for _tok in ("keysv", "rwv", "rbv"):
            tt.note_write(_tok, "G", ctx.cnt["G"])
        tt.op("vector", lambda E: E.tensor_tensor(kkv, keysv, keysv, op=ALU.mult),
              reads=["krw_done"], writes=["kkv"])
        tt.op("vector", lambda E: E.scalar_tensor_tensor(
            keysv, keysv, 2.0, rwv, op0=ALU.mult, op1=ALU.add),
            reads=["krw_done", "kkv"], writes=["keysv"])
        tt.op("vector", lambda E: E.tensor_reduce(
            rwv[:, 0:1], kkv[:, 0:1024], axis=AX.X, op=ALU.add),
            reads=["kkv", "keysv"], writes=["kredv"])
        for dc in range(8):
            tt.op("tensor", lambda E, dc=dc: E.transpose(
                P_(0)[:, 0:16], keysv[:, dc * 128:(dc + 1) * 128],
                eye_f32[0:16, 0:16]),
                reads=["keysv", "consts"], writes=["pb0"])
            tt.op("vector", lambda E, dc=dc: E.tensor_copy(
                RT_bf[:, dc * 16:(dc + 1) * 16], P_(0)[:, 0:16]),
                reads=["pb0"], writes=["router_w"])
        tt.op("tensor", lambda E: E.transpose(
            P_(0)[0:1, 0:16], rwv[0:16, 0:1], eye_f32[0:16, 0:16]),
            reads=["kredv", "consts"], writes=["pb0"])
        tt.op("vector", lambda E: E.tensor_tensor(
            cvec_row[:], rb_row[:], P_(0)[0:1, 0:16], op=ALU.subtract),
            reads=["pb0", "rbv"], writes=["router_w"])
        def fb1(E):
            return E.dma_start(
                out=b1all.rearrange("r (e c) -> r e c", e=NE),
                in_=d_b1.rearrange("e r c -> r e c"))
        tt.op("sync", fb1, writes=["b1nat"], sem="LB1", inc=16)
        for e in range(NE):
            tt.op("tensor", lambda E, e=e: E.transpose(
                P_(0)[:, 0:32], b1all[:, e * 128:(e + 1) * 128],
                eye_f32[0:32, 0:32]),
                reads=["b1nat", "consts"], writes=["pb0"])
            tt.op("vector", lambda E, e=e: E.tensor_copy(
                b1cols[:, e * 32:(e + 1) * 32], P_(0)[:, 0:32]),
                reads=["pb0"], writes=["b1cols_tok"])

        ARENA_TOKS = (["KN%d" % i for i in range(16)]
                      + ["Vw%d" % i for i in range(16)]
                      + ["Gw%d" % i for i in range(16)]
                      + ["QTw_buf", "keysv", "rwv", "kkv", "kredv", "b1nat"]
                      + ["%s%d" % (n, p) for p in range(2) for n in
                         ("ratio", "ratioT", "A", "AT", "P", "N", "NT", "U",
                          "m1_", "m2_", "m3_", "KnT", "UT", "Usc", "DmT",
                          "OG", "OGT")])

        def arena_fence(pairs):
            for tok in ARENA_TOKS:
                for sm, cn in pairs:
                    if cn > 0:
                        tt.note_read(tok, sm, cn)

        arena_fence([("V", ctx.cnt["V"]), ("T", ctx.cnt["T"]),
                     ("G", ctx.cnt["G"])])

        def load_tile(w, ti):
            s, k = divmod(ti, 8)
            g = ti % 4
            src = d_x[s, w * 1024 + k * 128: w * 1024 + (k + 1) * 128, :]
            dst = acc[:, ti * 1024:(ti + 1) * 1024]
            extra = ["consts"] if (w == 0 and ti == 0) else []
            tt.op("gpsimd", lambda E, src=src, dst=dst: E.dma_start(
                out=dst, in_=src),
                reads=extra, writes=["acc%d" % ti, "lxq%d" % g],
                sem="LX%d" % g, inc=16)

        def store_tile(w, ti):
            s, k = divmod(ti, 8)
            g = ti % 4
            dst = d_out[s, w * 1024 + k * 128: w * 1024 + (k + 1) * 128, :]
            src = acc[:, ti * 1024:(ti + 1) * 1024]
            tt.op("gpsimd", lambda E, dst=dst, src=src: E.dma_start(
                out=dst, in_=src),
                reads=["acc%d" % ti], writes=["stq%d" % g],
                sem="ST%d" % g, inc=16)

        prev_store = None
        for w in range(2):
            tgate = emit_attn_wave(tt, B, w, NT=NT,
                                   load_tile=lambda ti, w=w: load_tile(w, ti),
                                   store_prev=prev_store)
            ctx.wait("tensor", "TXT0", ctx.cnt["TXT0"])
            ctx.wait("tensor", "TXT1", ctx.cnt["TXT1"])
            ctx.wait("tensor", "V", ctx.cnt["V"])
            ctx.wait("tensor", "GE", ctx.cnt["GE"])
            ctx.wait("scalar", "V", ctx.cnt["V"])
            ctx.wait("vector", "V", ctx.cnt["V"])
            ctx.wait("vector", "GE", ctx.cnt["GE"])
            t_s1g, t_s2t, v_drain, npairs = emit_moe_wave(ctx, B, NE, NT,
                                                          gate=tgate)
            t_last = t_s2t[(npairs - 1, NTILE - 1)]
            t_free_ph = t_s1g[(npairs - 1, 4 * (NT // TH) - 1)]
            v_last = v_drain[(npairs - 1, NTILE - 1)]
            ge_last = ctx.cnt["GE"]
            for ti in range(NTILE):
                tt.note_write("acc%d" % ti, "V", v_drain[(npairs - 1, ti)])
            for ti in range(NTILE):
                tt.note_read("xT_%d" % ti, "T", t_free_ph)
            for ti in range(NTILE):
                tt.note_read("wrt%d" % ti, "V", v_last)
            tt.note_read("b1cols_tok", "GE", ge_last)
            tt.note_read("b2row_tok", "T", t_last)
            arena_fence([("T", t_last), ("V", v_last), ("GE", ge_last)])
            for i in range(4):
                tt.note_write("pb%d" % i, "T", t_free_ph)
                tt.note_read("pb%d" % i, "GE", ge_last)
            for i in range(4, 8):
                tt.note_write("pb%d" % i, "T", t_last)
                tt.note_read("pb%d" % i, "V", v_last)
                tt.note_read("pb%d" % i, "GE", ge_last)
            prev_store = lambda ti, w=w: store_tile(w, ti)
        for ti in range(NTILE):
            prev_store(ti)
        for g in range(4):
            ctx.wait("gpsimd", "ST%d" % g, ctx.cnt["ST%d" % g])
        ctx.replay(block)
    return nc


_NC_CACHE = {}


def kernel(**inputs):
    NE = 16
    x = np.ascontiguousarray(np.asarray(inputs["L0_attn_x"], np.float32))
    w1 = np.asarray(inputs["L0_ffn_w1"], np.float32)
    w2 = np.asarray(inputs["L0_ffn_w2"], np.float32)
    w1r = np.ascontiguousarray(
        w1.reshape(NE, 8, 512, 8, 128).transpose(0, 1, 4, 3, 2))
    w2r = np.ascontiguousarray(
        w2.reshape(NE, 1024, 8, 4, 128).transpose(0, 2, 4, 3, 1))
    b1 = np.ascontiguousarray(np.asarray(inputs["L0_ffn_b1"], np.float32)
                              .reshape(NE, 32, 128))
    b2 = np.ascontiguousarray(np.asarray(inputs["L0_ffn_b2"], np.float32)
                              .reshape(1, NE * 1024))
    wqkvg = np.concatenate([
        np.asarray(inputs["L0_attn_Wq"], np.float32),
        np.asarray(inputs["L0_attn_Wk"], np.float32),
        np.asarray(inputs["L0_attn_Wv"], np.float32),
        np.asarray(inputs["L0_attn_Wg"], np.float32)], axis=1)
    wqkvg_r = np.ascontiguousarray(wqkvg.reshape(8, 128, 512).transpose(1, 0, 2))
    wab = np.concatenate([np.asarray(inputs["L0_attn_Wa"], np.float32),
                          np.asarray(inputs["L0_attn_Wb"], np.float32)], axis=1)
    wab_r = np.ascontiguousarray(wab.reshape(8, 128, 2).transpose(1, 0, 2))
    wo = np.ascontiguousarray(np.asarray(inputs["L0_attn_Wo"], np.float32))
    bqkvg = np.ascontiguousarray(np.concatenate(
        [np.asarray(inputs["L0_attn_bq"], np.float32),
         np.asarray(inputs["L0_attn_bk"], np.float32),
         np.asarray(inputs["L0_attn_bv"], np.float32),
         np.asarray(inputs["L0_attn_bg"], np.float32)]).reshape(1, 512))
    bab = np.ascontiguousarray(np.concatenate(
        [np.asarray(inputs["L0_attn_ba"], np.float32),
         np.asarray(inputs["L0_attn_bb"], np.float32)]).reshape(1, 2))
    bo = np.ascontiguousarray(np.asarray(inputs["L0_attn_bo"], np.float32)
                              .reshape(1, 1024))
    keys = np.ascontiguousarray(np.asarray(inputs["L0_ffn_keys"], np.float32))
    rw = np.ascontiguousarray(np.asarray(inputs["L0_ffn_rw"], np.float32))
    rb = np.ascontiguousarray(np.asarray(inputs["L0_ffn_rb"], np.float32)
                              .reshape(1, 16))

    if "nc" not in _NC_CACHE:
        _NC_CACHE["nc"] = build(NE)
    nc = _NC_CACHE["nc"]

    shared = {"w1r": w1r, "w2r": w2r, "b1": b1, "b2": b2,
              "wqkvg": wqkvg_r, "wab": wab_r, "wo": wo, "bqkvg": bqkvg,
              "bab": bab, "bo": bo, "keys": keys, "rw": rw, "rb": rb}
    in_maps = []
    for c in range(8):
        m = dict(shared)
        m["x"] = np.ascontiguousarray(x[2 * c:2 * c + 2])
        in_maps.append(m)
    res = run_bass_kernel_spmd(nc, in_maps, core_ids=list(range(8)))
    out = np.concatenate([r["out"] for r in res.results], axis=0)
    return out
